# version 07
# baseline (speedup 1.0000x reference)
import numpy as np
from contextlib import ExitStack
import concourse.bass as bass
import concourse.mybir as mybir
from concourse.bass_utils import run_bass_kernel_spmd

F32 = mybir.dt.float32
BF16 = mybir.dt.bfloat16
I32 = mybir.dt.int32
AF = mybir.ActivationFunctionType
ALU = mybir.AluOpType
AX = mybir.AxisListType

B, S, D, H, DH, FF, L = 8, 2048, 1024, 8, 128, 2048, 4
NCH = S // 128
NHB = 16
EPS = 1e-5

_STAGE = "full"
_SIM_BLEND = False

ENGINES = ("sync", "gpsimd", "vector", "scalar", "tensor")
_OWN = {"sync": "sd", "gpsimd": "gd", "vector": "ve", "scalar": "ac", "tensor": "pe"}


class Prog:
    def __init__(self, nc):
        self.nc = nc
        self.q = {e: [] for e in ENGINES}
        self.cnt = {}
        self.sems = {}

    def _ensure(self, s):
        if s not in self.cnt:
            self.cnt[s] = 0

    def rec(self, eng, fn, waits=(), inc=None, inc_n=1):
        if inc is not None:
            self._ensure(inc)
            self.cnt[inc] += inc_n
        for (s, v) in waits:
            self._ensure(s)
        self.q[eng].append({"fn": fn, "waits": [w for w in waits if w[1] > 0],
                            "inc": inc, "inc_n": inc_n})
        return (inc, self.cnt[inc]) if inc else None

    def tok(self, s):
        self._ensure(s)
        return (s, self.cnt[s])

    def dma(self, eng, out, in_, waits=(), sem=None):
        s = sem or ("sd" if eng == "sync" else "gd")
        return self.rec(eng, lambda e, out=out, in_=in_: e.dma_start(out=out, in_=in_),
                        waits=waits, inc=s, inc_n=16)

    def barrier(self):
        toks = [self.tok(s) for s in self.cnt]
        for e in ENGINES:
            own = _OWN[e]
            w = [t for t in toks if t[0] != own and t[1] > 0]
            if w:
                self.rec(e, None, waits=w)

    def emit(self, block):
        def make(eng_name):
            drain_eng = eng_name in ("vector", "scalar")
            def run(eng):
                for r in self.q[eng_name]:
                    for (s, v) in r["waits"]:
                        eng.wait_ge(self.sems[s], v)
                    if r["fn"] is None:
                        continue
                    inst = r["fn"](eng)
                    if r["inc"] is not None:
                        if drain_eng and r["inc_n"] == 1:
                            eng.drain().then_inc(self.sems[r["inc"]], r["inc_n"])
                        else:
                            inst.then_inc(self.sems[r["inc"]], r["inc_n"])
            return run
        block.sync(make("sync"))
        block.gpsimd(make("gpsimd"))
        block.vector(make("vector"))
        block.scalar(make("scalar"))
        block.tensor(make("tensor"))


def _host_tables():
    t = {}
    pos = np.arange(S, dtype=np.float32)
    expo = np.arange(0, DH, 2).astype(np.float32) / np.float32(DH)
    inv_freq = (np.float32(1.0) / np.power(np.float32(10000.0), expo,
                                           dtype=np.float32)).astype(np.float32)
    ang = (pos[:, None] * inv_freq[None, :]).astype(np.float32).astype(np.float64)

    def to_chunks(a):
        a = a.reshape(NCH, 128, DH // 2).transpose(1, 0, 2).reshape(128, NCH * (DH // 2))
        return np.ascontiguousarray(a.astype(np.float32))
    t["cos_tab"] = to_chunks(np.cos(ang))
    t["sin_tab"] = to_chunks(np.sin(ang))
    lin = np.linspace(np.log(np.float64(1 / 32)), np.log(np.float64(1 / 512)), H)
    gammas32 = (np.float32(1.0) - np.exp(lin.astype(np.float32))).astype(np.float32)
    g = gammas32.astype(np.float64)
    pp = np.arange(128, dtype=np.float64)
    t["gamq"] = np.ascontiguousarray((g[None, :] ** (pp[:, None] + 1)).astype(np.float32))
    t["gamk2"] = np.ascontiguousarray((g[None, :] ** (-(pp[:, None] + 1))).astype(np.float32))
    t["gk128"] = np.ascontiguousarray(
        np.broadcast_to((g ** 128).astype(np.float32)[None, :], (128, H)).copy())
    t["gc128"] = [float(v) for v in (g ** 128).astype(np.float32)]
    m = np.arange(128)
    t["maskT"] = np.ascontiguousarray((m[None, :] >= m[:, None]).astype(np.float32))
    t["identf"] = np.eye(128, dtype=np.float32)
    ones2 = np.zeros((128, 2), dtype=np.float32)
    ones2[0:64, 0] = 1.0
    ones2[64:128, 1] = 1.0
    t["ones2"] = ones2
    sel2 = np.zeros((2, 128), dtype=np.float32)
    sel2[0, 0:64] = 1.0
    sel2[1, 64:128] = 1.0
    t["sel2"] = sel2
    return t


def _perm_cols(w):
    w = np.asarray(w)
    perm = np.concatenate([np.arange(0, 128, 2), np.arange(1, 128, 2)])
    idx = np.concatenate([h * 128 + perm for h in range(H)])
    return np.ascontiguousarray(w[:, idx].astype(np.float32))


def build_program(stage, tabs):
    nc = bass.Bass("TRN2", target_bir_lowering=False)
    dram = {}

    def din(name, shape):
        dram[name] = nc.dram_tensor(name, list(shape), F32, kind="ExternalInput").ap()
        return dram[name]

    x0 = din("x0", (S, D))
    for i in range(L):
        for nm, sh in [("wq", (D, D)), ("wk", (D, D)), ("wv", (D, D)), ("wg", (D, D)),
                       ("wo", (D, D)), ("w1", (D, FF)), ("w2", (FF, D))]:
            din(f"{nm}{i}", sh)
    din("ln_s", (8, 128)); din("ln_b", (8, 128))
    din("gn_s", (L, 8, 128)); din("gn_b", (L, 8, 128))
    din("cos_tab", (128, NCH * 64)); din("sin_tab", (128, NCH * 64))
    din("gamq", (128, H)); din("gamk2", (128, H)); din("gk128", (128, H))
    din("maskT", (128, 128)); din("identf", (128, 128))
    din("ones2", (128, 2)); din("sel2", (2, 128))
    din("wbq", (D, D)); din("wbk", (D, D)); din("wbv", (D, D)); din("wbo", (D, D))
    din("bbv", (1, D)); din("bbo", (1, D)); din("bf", (1, D))
    din("bbq8", (8, 128)); din("bbk8", (8, 128))
    din("r2", (128, 128))
    din("wf", (D, D))
    out = nc.dram_tensor("out", [S, D], F32, kind="ExternalOutput").ap()
    cc_in = nc.dram_tensor("cc_in", [1, 1], F32, kind="Internal")
    cc_out = nc.dram_tensor("cc_out", [1, 1], F32, kind="Internal")

    es = ExitStack()
    with es:
        def sbuf(name, shape, dt=F32):
            return es.enter_context(nc.sbuf_tensor(name, list(shape), dt))

        xs = sbuf("xs", (128, NCH * 1024))
        wpool = sbuf("wpool", (128, 5 * 8192), BF16)
        BP = sbuf("BP", (128, 9472), BF16)
        ropeA = sbuf("ropeA", (128, 1024))
        ropeB = sbuf("ropeB", (128, 1024))
        sqscr = sbuf("sqscr", (128, 1024))
        xnf = sbuf("xnf", (128, 1024))
        QI2 = sbuf("QI2", (128, 1024), I32)
        MKI = sbuf("MKI", (128, 1024), I32)
        tabq = sbuf("tabq", (128, 1024), BF16)
        Sst = sbuf("Sst", (128, 1024))
        S16 = sbuf("S16", (128, 1024), BF16)
        gnbg = sbuf("gnbg", (128, 1024), BF16)
        rowbuf = sbuf("rowbuf", (1, 4096), BF16)
        gamqS = sbuf("gamqS", (128, H))
        gamk2S = sbuf("gamk2S", (128, H))
        gk128S = sbuf("gk128S", (128, H))
        maskS = sbuf("maskS", (128, 128), BF16)
        idfS = sbuf("idfS", (128, 128))
        idbS = sbuf("idbS", (128, 128), BF16)

        lncol = sbuf("lncol", (128, 24))
        gncol = sbuf("gncol", (128, 8))
        statsS = sbuf("statsS", (128, 16))
        mv = sbuf("mv", (128, 8))
        gnm = sbuf("gnm", (128, 32))


        onesf = sbuf("onesf", (1, 128))
        onesb = sbuf("onesb", (1, 128), BF16)
        bcol16 = sbuf("bcol16", (128, 8), BF16)
        scl = sbuf("scl", (128, 4))
        sclr = sbuf("sclr", (1, 8))



        WREG = [wpool[:, r * 8192:(r + 1) * 8192] for r in range(5)]
        wq8 = WREG[2]
        wf32 = wpool[:, 0:16384].bitcast(F32)
        xht = BP[:, 0:1024]
        xhf = BP[:, 1024:2048]
        qtm = BP[:, 2048:3072]
        ktm2 = BP[:, 3072:4096]
        ktms = BP[:, 4096:5120]
        vtm = BP[:, 5120:6144]
        gtm = BP[:, 6144:7168]
        qf = BP[:, 7168:8192]
        k2f = BP[:, 8192:9216]
        attm2 = BP[:, 9216:9472]
        h1f = BP[:, 2048:4096]
        kvq = BP[:, 4096:5152]
        xq8 = qtm
        xq8f = qf
        gyfm = k2f
        gybuf = QI2[:, 0:1024].bitcast(BF16)[:, 0:1024]
        kvacc = Sst[:, 0:512]
        rows8 = xnf
        tabk = tabq
        r2b = BP[:, 9216:9344]
        stashS = Sst[:, 642:674]
        amaxc = gnm[:, 0:16]
        bcolS = lncol[:, 0:16]
        sel2S = Sst[0:2, 512:640]
        ones2S = Sst[:, 640:642]
        coslot = MKI.bitcast(F32)[:, 0:64]
        sinlot = MKI.bitcast(F32)[:, 64:128]

        pdef = lambda name, shape, dt=F32: es.enter_context(nc.psum_tensor(name, list(shape), dt))
        pA = pdef("pA", (128, 512))
        pB = pdef("pB", (128, 512))
        pY = pdef("pY", (128, 1024))
        pT = pdef("pT", (128, 1024), BF16)
        pAt1 = pdef("pAt1", (128, 512))
        pAt2 = pdef("pAt2", (128, 512))
        pR = pdef("pR", (128, 512))

        p = Prog(nc)

        for name, dst in [("gamq", gamqS), ("gamk2", gamk2S), ("gk128", gk128S),
                          ("identf", idfS)]:
            p.dma("sync", dst[:], dram[name])
        p.dma("gpsimd", idbS[:], dram["identf"])
        p.dma("gpsimd", maskS[:], dram["maskT"])
        for c in range(NCH):
            p.dma("sync", xs[:, c * 1024:(c + 1) * 1024], x0[c * 128:(c + 1) * 128, :])
        p.rec("vector", lambda e: e.memset(onesf[:], 1.0), inc="ve")
        p.rec("vector", lambda e: e.memset(onesb[:], 1.0), inc="ve")

        def prep_vec_cols(rows_ap, out_cols_ap, n=8, tok_in=None):
            tw = [p.tok("ve"), p.tok("ac")]
            if tok_in is not None:
                tw.append(tok_in)
            p.rec("tensor", lambda e, r=rows_ap, n=n: e.transpose(
                pR[:, 0:n], r, idfS[0:n, 0:n]), waits=tw, inc="pe")
            tpe = p.tok("pe")
            p.rec("vector", lambda e, o=out_cols_ap, n=n: e.tensor_copy(o, pR[:, 0:n]),
                  waits=[tpe], inc="ve")
            return p.tok("ve")

        p.barrier()
        t_r = p.dma("sync", rows8[0:8, 0:128], dram["ln_s"], sem="rowd")
        prep_vec_cols(rows8[0:8, 0:128], lncol[:, 0:8], tok_in=t_r)
        t_r = p.dma("sync", rows8[0:8, 0:128], dram["ln_b"], waits=[p.tok("ve")], sem="rowd")
        prep_vec_cols(rows8[0:8, 0:128], lncol[:, 8:16], tok_in=t_r)
        p.rec("vector", lambda e: e.reciprocal(lncol[:, 16:24], lncol[:, 0:8]), inc="ve")
        p.rec("vector", lambda e: e.tensor_mul(lncol[:, 16:24], lncol[:, 16:24],
                                               lncol[:, 8:16]), inc="ve")
        p.rec("vector", lambda e: e.tensor_copy(bcol16[:, 0:8], lncol[:, 16:24]), inc="ve")
        p.barrier()

        def ln_stats(src_ap, rstd_col, negmur_col):
            p.rec("vector", lambda e, s=src_ap: e.bn_stats(statsS[:, 0:6], s[:, 0:512]), inc="ve")
            p.rec("vector", lambda e, s=src_ap: e.bn_stats(statsS[:, 6:12], s[:, 512:1024]), inc="ve")
            p.rec("vector", lambda e: e.bn_aggr(mv[:, 0:2], statsS[:, 0:12]), inc="ve")
            p.rec("vector", lambda e: e.tensor_single_scalar(mv[:, 2:3], mv[:, 1:2], EPS, ALU.add), inc="ve")
            tva = p.tok("ve")
            p.rec("scalar", lambda e: e.activation(mv[:, 3:4], mv[:, 2:3], AF.Sqrt),
                  waits=[tva], inc="ac")
            tac = p.tok("ac")
            p.rec("vector", lambda e, r=rstd_col: e.reciprocal(r, mv[:, 3:4]), waits=[tac], inc="ve")
            p.rec("vector", lambda e, r=rstd_col, nm2=negmur_col: e.tensor_scalar(
                nm2, mv[:, 0:1], r, -1.0, ALU.mult, ALU.mult), inc="ve")
            return p.tok("ve")

        def transpose8(src_sb, dst_sb, src_cols=None, dst_cols=None, n=8, extra_waits=()):
            tw = list(extra_waits) + [p.tok("ac")]
            for j in range(n):
                sc = src_cols(j) if src_cols else (j * 128, (j + 1) * 128)
                p.rec("tensor", lambda e, sc=sc, j=j, s=src_sb: e.transpose(
                    pT[:, j * 128:(j + 1) * 128], s[:, sc[0]:sc[1]], idbS[:]),
                    waits=tw if j == 0 else (), inc="pe")
            tpe = p.tok("pe")
            for j in range(n):
                dc = dst_cols(j) if dst_cols else (j * 128, (j + 1) * 128)
                p.rec("scalar", lambda e, dc=dc, j=j, d=dst_sb: e.copy(
                    d[:, dc[0]:dc[1]], pT[:, j * 128:(j + 1) * 128]),
                    waits=[tpe] if j == 0 else (), inc="ac")
            return p.tok("ac")

        def load_weight_scaled(dst_region, src_dram, scale_col_fn, nslab, slab_w=1024):
            n_pieces = nslab * (slab_w // 1024)
            for pc in range(n_pieces):
                j = pc // (slab_w // 1024)
                off = (pc % (slab_w // 1024)) * 1024
                buf = ropeA if pc % 2 == 0 else ropeB
                w = [p.tok("ve")]
                td = p.dma("sync", buf[:, 0:1024],
                           src_dram[j * 128:(j + 1) * 128, off:off + 1024], waits=w,
                           sem=f"wl{pc % 2}")
                sc = scale_col_fn(j)
                p.rec("vector", lambda e, buf=buf, j=j, off=off, sc=sc, d=dst_region, sw=slab_w:
                      e.tensor_scalar_mul(d[:, j * sw + off:j * sw + off + 1024], buf[:, 0:1024], sc),
                      waits=[td], inc="ve")
            return p.tok("ve")

        def load_weight_cast(dst_region, src_dram, nslab, slab_w=1024):
            tok = None
            for j in range(nslab):
                tok = p.dma("gpsimd", dst_region[:, j * slab_w:(j + 1) * slab_w],
                            src_dram[j * 128:(j + 1) * 128, 0:slab_w])
            return tok

        def beta_row(weight_region, out_col0, width=1024, nchunk=8, slab_w=1024):
            for s_ in range((width + 511) // 512):
                w0, w1 = s_ * 512, min(width, s_ * 512 + 512)
                for j in range(nchunk):
                    wts = [p.tok("ve")] if j == 0 else ()
                    p.rec("tensor", lambda e, j=j, w0=w0, w1=w1, wr=weight_region, sw=slab_w:
                          e.matmul(pR[0:1, 0:w1 - w0], bcol16[:, j:j + 1],
                                   wr[:, j * sw + w0:j * sw + w1],
                                   start=(j == 0), stop=(j == nchunk - 1)), waits=wts, inc="pe")
                tpe = p.tok("pe")
                p.rec("vector", lambda e, w0=w0, w1=w1, o=out_col0: e.tensor_copy(
                    rowbuf[0:1, o + w0:o + w1], pR[0:1, 0:w1 - w0]),
                    waits=[tpe], inc="ve")
            return p.tok("ve")

        def tm_proj(psum_half, half, wregion, xf_tiles, bias_row_col0, extra_waits=(),
                    auto=True):
            ew = list(extra_waits) + ([p.tok("ve"), p.tok("ac")] if auto else [])
            if bias_row_col0 is not None:
                b0 = bias_row_col0 + half * 512
                p.rec("tensor", lambda e, b0=b0, ph=psum_half: e.matmul(
                    ph, onesb[0:1, :], rowbuf[0:1, b0:b0 + 512],
                    start=True, stop=False), waits=ew, inc="pe")
                first = False
            else:
                first = True
            for j in range(8):
                st = first and (j == 0)
                p.rec("tensor", lambda e, j=j, st=st, ph=psum_half, wr=wregion, xf=xf_tiles, half=half:
                      e.matmul(ph, xf(j), wr[:, j * 1024 + half * 512:j * 1024 + half * 512 + 512],
                               start=st, stop=(j == 7)),
                      waits=ew if st else (), inc="pe")
            return p.tok("pe")

        def quant(src_f32, scale_col, out_bf16, width=1024):
            QI1v = sqscr.bitcast(I32)
            p.rec("vector", lambda e, s=src_f32, sc=scale_col: e.tensor_scalar_mul(s, s, sc), inc="ve")
            u = src_f32.bitcast(I32)
            p.rec("vector", lambda e, u=u: e.tensor_single_scalar(QI1v[:, 0:width], u, 20,
                                                                  ALU.logical_shift_right), inc="ve")
            p.rec("vector", lambda e, u=u: e.tensor_single_scalar(QI2[:, 0:width], u, 0xFFFFF,
                                                                  ALU.bitwise_and), inc="ve")
            p.rec("vector", lambda e: e.tensor_single_scalar(MKI[:, 0:width], QI1v[:, 0:width],
                                                             1, ALU.bitwise_and), inc="ve")
            p.rec("vector", lambda e: e.tensor_add(QI2[:, 0:width], MKI[:, 0:width],
                                                   QI2[:, 0:width]), inc="ve")
            p.rec("vector", lambda e: e.scalar_tensor_tensor(QI2[:, 0:width], QI2[:, 0:width],
                                                             524289, QI1v[:, 0:width],
                                                             ALU.is_ge, ALU.add), inc="ve")
            p.rec("vector", lambda e: e.tensor_single_scalar(ropeA[:, 0:width].bitcast(I32),
                                                             QI2[:, 0:width], 20,
                                                             ALU.logical_shift_left), inc="ve")
            p.rec("vector", lambda e: e.tensor_single_scalar(MKI[:, 0:width], QI1v[:, 0:width],
                                                             0x7FF, ALU.bitwise_and), inc="ve")
            p.rec("vector", lambda e: e.tensor_single_scalar(MKI[:, 0:width], MKI[:, 0:width],
                                                             968, ALU.is_ge), inc="ve")
            p.rec("vector", lambda e, s=src_f32: e.tensor_scalar(ropeB[:, 0:width], s,
                                                                 24576.0, -24576.0,
                                                                 ALU.add, ALU.add), inc="ve")
            if _SIM_BLEND:
                p.rec("vector", lambda e: e.tensor_copy(sqscr[:, 0:width], MKI[:, 0:width]), inc="ve")
                p.rec("vector", lambda e: e.tensor_tensor(ropeA[:, 0:width], ropeA[:, 0:width],
                                                          ropeB[:, 0:width], ALU.subtract), inc="ve")
                p.rec("vector", lambda e: e.tensor_tensor(ropeA[:, 0:width], ropeA[:, 0:width],
                                                          sqscr[:, 0:width], ALU.mult), inc="ve")
                p.rec("vector", lambda e, o=out_bf16: e.tensor_tensor(o, ropeA[:, 0:width],
                                                                      ropeB[:, 0:width], ALU.add), inc="ve")
            else:
                p.rec("vector", lambda e, o=out_bf16: e.select(o, MKI[:, 0:width], ropeA[:, 0:width],
                                                               ropeB[:, 0:width]), inc="ve")
            return p.tok("ve")

        def partition_amax_to_cell(col_ap, out_cell):
            tw = [p.tok("ve")]
            p.rec("tensor", lambda e, c=col_ap: e.transpose(pR[0:1, 0:128], c, idfS[:]),
                  waits=tw, inc="pe")
            tpe = p.tok("pe")
            p.rec("vector", lambda e, o=out_cell: e.tensor_reduce(o, pR[0:1, 0:128], AX.X, ALU.max),
                  waits=[tpe], inc="ve")
            return p.tok("ve")

        def bcast_cell_to_col(cell_ap, col_out):
            tw = [p.tok("ve")]
            p.rec("tensor", lambda e, c=cell_ap: e.matmul(pR[:, 0:1], onesf[0:1, :], c,
                                                          start=True, stop=True), waits=tw, inc="pe")
            tpe = p.tok("pe")
            p.rec("vector", lambda e, o=col_out: e.tensor_copy(o, pR[:, 0:1]), waits=[tpe], inc="ve")
            return p.tok("ve")

        def emit_out():
            p.barrier()
            for c in range(NCH):
                p.dma("sync", out[c * 128:(c + 1) * 128, :], xs[:, c * 1024:(c + 1) * 1024])
            p.rec("sync", None, waits=[p.tok("sd")])
            for s in list(p.cnt):
                p.sems[s] = es.enter_context(nc.semaphore(s))
            nums = sorted(h.num for h in p.sems.values())
            srange = range(nums[0], nums[-1] + 1)
            nc.all_engine_barrier()
            nc.gpsimd.dma_reset(srange)
            nc.gpsimd.sem_clear(srange)
            nc.all_engine_barrier()
            p.emit(es.enter_context(nc.Block()))
            return nc

        for c in range(NCH):
            xc = xs[:, c * 1024:(c + 1) * 1024]
            ln_stats(xc, mv[:, 4:5], mv[:, 5:6])
            if stage.startswith("ln1dbg"):
                p.rec("vector", lambda e, c=c: e.tensor_copy(stashS[:, 2*c:2*c+2], mv[:, 4:6]), inc="ve")
            if stage != "ln1dbg2":
                p.rec("vector", lambda e, xc=xc: e.tensor_scalar(
                    xc, xc, mv[:, 4:5], mv[:, 5:6], ALU.mult, ALU.add), inc="ve")
        p.barrier()
        if stage.startswith("ln1dbg"):
            p.rec("vector", lambda e: e.tensor_copy(ropeA[:, 0:32], stashS[:, 0:32]), inc="ve")
            p.barrier()
            p.rec("vector", lambda e: e.tensor_copy(xs[:, 0:32], ropeA[:, 0:32]), inc="ve")
        if stage in ("ln1", "ln1dbg", "ln1dbg2"):
            return emit_out()

        n_layers, upto = L, "full"
        if stage.startswith("attnA"):
            _, k_, upto = stage.split(":")
            n_layers = int(k_)
        if stage.startswith("dbg"):
            n_layers = int(stage[-1]) if stage[-1].isdigit() else 1

        for li in range(n_layers):
            lupto = upto if (li == n_layers - 1) else "full"
            if stage.rstrip("0123456789") == "dbgx" and li == n_layers - 1:
                return emit_out()
            for wi, nm in enumerate(["wq", "wk", "wv", "wg"]):
                load_weight_scaled(WREG[wi], dram[f"{nm}{li}"], lambda j: lncol[:, j:j + 1], 8)
            p.barrier()
            for wi in range(4):
                beta_row(WREG[wi], wi * 1024)
            t_r = p.dma("sync", rows8[0:8, 0:128], dram["gn_s"][li], waits=[p.tok("ve")],
                        sem="rowd")
            prep_vec_cols(rows8[0:8, 0:128], gncol[:, 0:8], tok_in=t_r)
            t_r2 = p.dma("sync", rows8[0:8, 128:256], dram["gn_b"][li],
                         waits=[p.tok("ve")], sem="rowd")
            p.rec("vector", lambda e: e.reciprocal(rows8[0:8, 0:128], rows8[0:8, 0:128]),
                  waits=[t_r2], inc="ve")
            p.rec("vector", lambda e: e.tensor_mul(rows8[0:8, 0:128], rows8[0:8, 0:128],
                                                   rows8[0:8, 128:256]), inc="ve")
            tvv = p.tok("ve")
            trow = p.dma("sync", ropeA[0:1, 0:1024], rows8[0:8, 0:128], waits=[tvv],
                         sem="rowd")
            for s_ in range(2):
                p.rec("tensor", lambda e, s_=s_: e.matmul(
                    pY[:, s_ * 512:(s_ + 1) * 512], onesf[0:1, :],
                    ropeA[0:1, s_ * 512:(s_ + 1) * 512],
                    start=True, stop=True), waits=[trow] if s_ == 0 else (), inc="pe")
            tpe = p.tok("pe")
            p.rec("vector", lambda e: e.tensor_copy(gnbg[:], pY[:, 0:1024]), waits=[tpe], inc="ve")
            load_weight_scaled(WREG[4], dram[f"wo{li}"], lambda j: gncol[:, j:j + 1], 8)
            p.rec("vector", lambda e: e.memset(Sst[:], 0.0), inc="ve")
            p.rec("vector", lambda e: e.memset(S16[:], 0.0), inc="ve")
            p.barrier()

            tok_S16 = {h: p.tok("ve") for h in range(H)}
            ln_stats(xs[:, 0:1024], mv[:, 4:5], mv[:, 5:6])
            p.rec("vector", lambda e: e.tensor_scalar(
                xht[:], xs[:, 0:1024], mv[:, 4:5], mv[:, 5:6], ALU.mult, ALU.add),
                waits=[p.tok("pe")], inc="ve")
            tapply = p.tok("ve")
            tnorm_prev = p.tok("ve")
            tktab_prev = p.tok("ve")
            tSupd7_prev = p.tok("ve")
            t_ycopy_prev = p.tok("ac")
            pend = None
            for c in range(NCH):
                xc = xs[:, c * 1024:(c + 1) * 1024]
                tcs = p.dma("sync", coslot, dram["cos_tab"][:, c * 64:(c + 1) * 64],
                            waits=[tktab_prev], sem="cs")
                tcs = p.dma("sync", sinlot, dram["sin_tab"][:, c * 64:(c + 1) * 64],
                            sem="cs")
                transpose8(xht, xhf, extra_waits=[tapply])
                txhf = p.tok("ac")
                if c + 1 < NCH:
                    xn_ = xs[:, (c + 1) * 1024:(c + 2) * 1024]
                    ln_stats(xn_, mv[:, 4:5], mv[:, 5:6])
                    p.rec("vector", lambda e, xn_=xn_: e.tensor_scalar(
                        xht[:], xn_, mv[:, 4:5], mv[:, 5:6], ALU.mult, ALU.add),
                        waits=[txhf], inc="ve")
                    tapply = p.tok("ve")
                xf_tiles = lambda j: xhf[:, j * 128:(j + 1) * 128]
                cosc = coslot[:, 0:64].unsqueeze(1).broadcast_to((128, 8, 64))
                sinc = sinlot[:, 0:64].unsqueeze(1).broadcast_to((128, 8, 64))
                gq3 = gamqS[:, 0:8].unsqueeze(2).broadcast_to((128, 8, 64))
                gk3 = gamk2S[:, 0:8].unsqueeze(2).broadcast_to((128, 8, 64))
                tq3 = tabq[:, 0:512].rearrange("p (h f) -> p h f", h=8)
                tq3b = tabq[:, 512:1024].rearrange("p (h f) -> p h f", h=8)
                tk3 = tabk[:, 0:512].rearrange("p (h f) -> p h f", h=8)
                tk3b = tabk[:, 512:1024].rearrange("p (h f) -> p h f", h=8)
                p.rec("vector", lambda e, a=tq3, g=gq3, cc2=cosc: e.tensor_tensor(a, g, cc2, ALU.mult),
                      waits=[tcs], inc="ve")
                p.rec("vector", lambda e, a=tq3b, g=gq3, ss2=sinc: e.tensor_tensor(a, g, ss2, ALU.mult), inc="ve")
                ttab = p.tok("ve")

                tm_proj(pY[:, 0:512], 0, WREG[0], xf_tiles, 0,
                        extra_waits=[txhf, t_ycopy_prev], auto=False)
                tq = tm_proj(pY[:, 512:1024], 1, WREG[0], xf_tiles, 0, auto=False)
                pv = pY[:, 0:1024].rearrange("p (h s f) -> p h s f", h=8, s=2)
                x1, x2 = pv[:, :, 0, :], pv[:, :, 1, :]
                qv = qtm[:, 0:1024].rearrange("p (h s f) -> p h s f", h=8, s=2)
                q1, q2 = qv[:, :, 0, :], qv[:, :, 1, :]
                rA = ropeA[:, 0:512].rearrange("p (h f) -> p h f", h=8)
                rB = ropeB[:, 0:512].rearrange("p (h f) -> p h f", h=8)
                p.rec("vector", lambda e, x1=x1, rA=rA, t=tq3: e.tensor_tensor(rA, x1, t, ALU.mult),
                      waits=[tq, ttab], inc="ve")
                p.rec("vector", lambda e, x2=x2, rB=rB, t=tq3b: e.tensor_tensor(rB, x2, t, ALU.mult), inc="ve")
                p.rec("vector", lambda e, q1=q1, rA=rA, rB=rB: e.tensor_tensor(q1, rA, rB, ALU.subtract), inc="ve")
                p.rec("vector", lambda e, x1=x1, rA=rA, t=tq3b: e.tensor_tensor(rA, x1, t, ALU.mult), inc="ve")
                p.rec("vector", lambda e, x2=x2, rB=rB, t=tq3: e.tensor_tensor(rB, x2, t, ALU.mult), inc="ve")
                p.rec("vector", lambda e, q2=q2, rA=rA, rB=rB: e.tensor_tensor(q2, rA, rB, ALU.add), inc="ve")
                tropeq = p.tok("ve")
                if stage.rstrip("0123456789") == "dbgq" and li == n_layers - 1:
                    p.rec("vector", lambda e, xc=xc: e.tensor_copy(xc, qtm[:, 0:1024]), inc="ve")
                    tnorm_prev = tktab_prev = tSupd7_prev = p.tok("ve")
                    t_ycopy_prev = p.tok("ac")
                    continue

                p.rec("vector", lambda e, a=tk3, g=gk3, cc2=cosc: e.tensor_tensor(a, g, cc2, ALU.mult), inc="ve")
                p.rec("vector", lambda e, a=tk3b, g=gk3, ss2=sinc: e.tensor_tensor(a, g, ss2, ALU.mult), inc="ve")
                tktab_prev = p.tok("ve")
                tm_proj(pA[:, 0:512], 0, WREG[1], xf_tiles, 1024,
                        extra_waits=[txhf, tSupd7_prev], auto=False)
                tk = tm_proj(pB[:, 0:512], 1, WREG[1], xf_tiles, 1024, auto=False)
                for hf, psrc in ((0, pA), (1, pB)):
                    pvh = psrc[:, 0:512].rearrange("p (h s f) -> p h s f", h=4, s=2)
                    xh1, xh2 = pvh[:, :, 0, :], pvh[:, :, 1, :]
                    kvh = ktm2[:, hf * 512:hf * 512 + 512].rearrange(
                        "p (h s f) -> p h s f", h=4, s=2)
                    kh1, kh2 = kvh[:, :, 0, :], kvh[:, :, 1, :]
                    rAh = ropeA[:, hf * 256:hf * 256 + 256].rearrange("p (h f) -> p h f", h=4)
                    rBh = ropeB[:, hf * 256:hf * 256 + 256].rearrange("p (h f) -> p h f", h=4)
                    t3 = tabk[:, hf * 256:hf * 256 + 256].rearrange("p (h f) -> p h f", h=4)
                    t3b = tabk[:, 512 + hf * 256:512 + hf * 256 + 256].rearrange(
                        "p (h f) -> p h f", h=4)
                    p.rec("vector", lambda e, x1=xh1, rA=rAh, t=t3: e.tensor_tensor(rA, x1, t, ALU.mult),
                          waits=[tk] if hf == 0 else (), inc="ve")
                    p.rec("vector", lambda e, x2=xh2, rB=rBh, t=t3b: e.tensor_tensor(rB, x2, t, ALU.mult), inc="ve")
                    p.rec("vector", lambda e, k1=kh1, rA=rAh, rB=rBh: e.tensor_tensor(k1, rA, rB, ALU.add), inc="ve")
                    p.rec("vector", lambda e, x2=xh2, rA=rAh, t=t3: e.tensor_tensor(rA, x2, t, ALU.mult), inc="ve")
                    p.rec("vector", lambda e, x1=xh1, rB=rBh, t=t3b: e.tensor_tensor(rB, x1, t, ALU.mult), inc="ve")
                    p.rec("vector", lambda e, k2=kh2, rA=rAh, rB=rBh: e.tensor_tensor(k2, rA, rB, ALU.subtract), inc="ve")
                tropek = p.tok("ve")
                g128v = gk128S[:, 0:8].unsqueeze(2).broadcast_to((128, 8, 128))
                p.rec("vector", lambda e, g=g128v: e.tensor_tensor(
                    ktms[:, 0:1024].rearrange("p (h f) -> p h f", h=8),
                    ktm2[:, 0:1024].rearrange("p (h f) -> p h f", h=8), g, ALU.mult), inc="ve")
                tktms = p.tok("ve")
                if stage.rstrip("0123456789") == "dbgk" and li == n_layers - 1:
                    p.rec("vector", lambda e, xc=xc: e.tensor_copy(xc, ktm2[:, 0:1024]), inc="ve")
                    tnorm_prev = tSupd7_prev = p.tok("ve")
                    t_ycopy_prev = p.tok("ac")
                    continue

                tm_proj(pY[:, 0:512], 0, WREG[2], xf_tiles, 2048,
                        extra_waits=[tropeq], auto=False)
                tv = tm_proj(pY[:, 512:1024], 1, WREG[2], xf_tiles, 2048, auto=False)
                p.rec("scalar", lambda e: e.copy(vtm[:, 0:1024], pY[:, 0:1024]),
                      waits=[tv], inc="ac")
                tvtm = p.tok("ac")
                if stage.rstrip("0123456789") == "dbgv" and li == n_layers - 1:
                    p.rec("vector", lambda e, xc=xc: e.tensor_copy(xc, vtm[:, 0:1024]),
                          waits=[tvtm], inc="ve")
                    tnorm_prev = tSupd7_prev = p.tok("ve")
                    t_ycopy_prev = p.tok("ac")
                    continue
                tm_proj(pA[:, 0:512], 0, WREG[3], xf_tiles, 3072,
                        extra_waits=[tropek], auto=False)
                tg = tm_proj(pB[:, 0:512], 1, WREG[3], xf_tiles, 3072, auto=False)
                p.rec("scalar", lambda e: e.activation(gtm[:, 0:512], pA[:, 0:512], AF.Silu),
                      waits=[tg], inc="ac")
                p.rec("scalar", lambda e: e.activation(gtm[:, 512:1024], pB[:, 0:512], AF.Silu),
                      inc="ac")
                tgtm = p.tok("ac")
                if stage.rstrip("0123456789") == "dbgg" and li == n_layers - 1:
                    p.rec("vector", lambda e, xc=xc: e.tensor_copy(xc, gtm[:, 0:1024]),
                          waits=[tgtm], inc="ve")
                    tnorm_prev = tSupd7_prev = p.tok("ve")
                    t_ycopy_prev = p.tok("ac")
                    continue

                if pend is not None:
                    pend()
                    pend = None
                transpose8(qtm, qf, extra_waits=[tropeq])
                tqf = p.tok("ac")
                transpose8(ktm2, k2f, extra_waits=[tropek])
                tkf = p.tok("ac")

                tok_mask = {}
                tok_Y = {}
                tok_Supd = {}
                for h in range(H):
                    patt = pAt1 if h % 2 == 0 else pAt2
                    aslot = attm2[:, (h % 2) * 128:(h % 2) * 128 + 128]
                    w_att = [tqf, tkf]
                    if h >= 2:
                        w_att.append(tok_mask[h - 2])
                    else:
                        w_att.append(p.tok("ve"))
                    p.rec("tensor", lambda e, h=h, patt=patt: e.matmul(
                        patt[:, 0:128], k2f[:, h * 128:(h + 1) * 128],
                        qf[:, h * 128:(h + 1) * 128], start=True, stop=True),
                        waits=w_att, inc="pe")
                    tattT = p.tok("pe")
                    p.rec("vector", lambda e, patt=patt, a=aslot: e.tensor_tensor(
                        a, patt[:, 0:128], maskS[:], ALU.mult),
                        waits=[tattT], inc="ve")
                    tok_mask[h] = p.tok("ve")
                    wy = [tok_mask[h], tvtm]
                    p.rec("tensor", lambda e, h=h, a=aslot, c=c: e.matmul(
                        pY[:, h * 128:(h + 1) * 128], a,
                        vtm[:, h * 128:(h + 1) * 128], start=True, stop=(c == 0)),
                        waits=wy, inc="pe")
                    if c > 0:
                        p.rec("tensor", lambda e, h=h: e.matmul(
                            pY[:, h * 128:(h + 1) * 128], qf[:, h * 128:(h + 1) * 128],
                            S16[:, h * 128:(h + 1) * 128], start=False, stop=True),
                            waits=[tok_S16[h]], inc="pe")
                    tok_Y[h] = p.tok("pe")
                    pst = pA if h % 2 == 0 else pB
                    ws = [tktms, tvtm, tgtm]
                    if h >= 2:
                        ws.append(tok_Supd[h - 2])
                    else:
                        ws.append(p.tok("ve"))
                    p.rec("tensor", lambda e, h=h, pst=pst: e.matmul(
                        pst[:, 0:128], ktms[:, h * 128:(h + 1) * 128],
                        vtm[:, h * 128:(h + 1) * 128], start=True, stop=True),
                        waits=ws, inc="pe")
                    tT = p.tok("pe")
                    gch = tabs["gc128"][h]
                    p.rec("vector", lambda e, h=h, pst=pst, gch=gch: e.scalar_tensor_tensor(
                        Sst[:, h * 128:(h + 1) * 128], Sst[:, h * 128:(h + 1) * 128],
                        gch, pst[:, 0:128], ALU.mult, ALU.add), waits=[tT], inc="ve")
                    tok_Supd[h] = p.tok("ve")
                    p.rec("vector", lambda e, h=h: e.tensor_copy(
                        S16[:, h * 128:(h + 1) * 128], Sst[:, h * 128:(h + 1) * 128]),
                        waits=[tok_Y[h]], inc="ve")
                    tok_S16[h] = p.tok("ve")

                tSupd7_prev = tok_Supd[H - 1]
                wgn = [tok_Y[h] for h in range(H)]
                if stage.rstrip("0123456789") == "dbgy" and li == n_layers - 1:
                    p.rec("vector", lambda e, xc=xc: e.tensor_copy(xc, pY[:, 0:1024]), waits=wgn, inc="ve")
                    tnorm_prev = p.tok("ve")
                    t_ycopy_prev = p.tok("ac")
                    continue
                p.rec("scalar", lambda e: e.copy(xnf[:, 0:1024], pY[:, 0:1024]),
                      waits=wgn + [tnorm_prev], inc="ac")
                t_ycopy_prev = p.tok("ac")
                p.rec("scalar", lambda e: e.activation(sqscr[:, 0:1024], xnf[:, 0:1024],
                                                       AF.Square), inc="ac")
                tsq = p.tok("ac")
                pv3 = xnf[:, 0:1024].rearrange("p (h f) -> p h f", h=8)
                sq3 = sqscr[:, 0:1024].rearrange("p (h f) -> p h f", h=8)
                p.rec("vector", lambda e, a=pv3: e.tensor_reduce(gnm[:, 0:8], a, AX.X, ALU.add),
                      waits=[t_ycopy_prev], inc="ve")
                p.rec("vector", lambda e, a=sq3: e.tensor_reduce(gnm[:, 8:16], a, AX.X, ALU.add),
                      waits=[tsq], inc="ve")
                p.rec("vector", lambda e: e.tensor_single_scalar(gnm[:, 0:8], gnm[:, 0:8],
                                                                 1.0 / 128, ALU.mult), inc="ve")
                p.rec("vector", lambda e: e.tensor_mul(statsS[:, 0:8], gnm[:, 0:8], gnm[:, 0:8]), inc="ve")
                p.rec("vector", lambda e: e.scalar_tensor_tensor(
                    gnm[:, 8:16], gnm[:, 8:16], 1.0 / 128, statsS[:, 0:8],
                    ALU.mult, ALU.subtract), inc="ve")
                p.rec("vector", lambda e: e.tensor_single_scalar(gnm[:, 8:16], gnm[:, 8:16],
                                                                 EPS, ALU.add), inc="ve")
                tve = p.tok("ve")
                p.rec("scalar", lambda e: e.activation(gnm[:, 16:24], gnm[:, 8:16], AF.Sqrt),
                      waits=[tve], inc="ac")
                tac = p.tok("ac")
                p.rec("vector", lambda e: e.reciprocal(gnm[:, 16:24], gnm[:, 16:24]),
                      waits=[tac], inc="ve")
                p.rec("vector", lambda e: e.tensor_mul(gnm[:, 24:32], gnm[:, 0:8], gnm[:, 16:24]), inc="ve")
                p.rec("vector", lambda e: e.tensor_single_scalar(gnm[:, 24:32], gnm[:, 24:32],
                                                                 -1.0, ALU.mult), inc="ve")
                for h in range(H):
                    p.rec("vector", lambda e, h=h: e.tensor_scalar(
                        ropeB[:, h * 128:(h + 1) * 128], xnf[:, h * 128:(h + 1) * 128],
                        gnm[:, 16 + h:17 + h], gnm[:, 24 + h:25 + h], ALU.mult, ALU.add), inc="ve")
                tnorm_prev = p.tok("ve")
                p.rec("vector", lambda e: e.tensor_add(ropeA[:, 0:1024], ropeB[:, 0:1024],
                                                       gnbg[:, 0:1024]), inc="ve")
                p.rec("vector", lambda e: e.tensor_mul(gybuf[:, 0:1024], ropeA[:, 0:1024],
                                                       gtm[:, 0:1024]), waits=[tgtm], inc="ve")
                tgy = p.tok("ve")
                DBGWO = (stage.rstrip("0123456789") == "dbgwo" and li == n_layers - 1)
                if stage.rstrip("0123456789") == "dbggy" and li == n_layers - 1:
                    p.rec("vector", lambda e, xc=xc: e.tensor_copy(xc, gybuf[:, 0:1024]), inc="ve")
                    continue

                def mk_epilogue(xc=xc, tgy=tgy, DBGWO=DBGWO):
                    def epi():
                        transpose8(gybuf, gyfm, extra_waits=[tgy])
                        tgyf = p.tok("ac")
                        for half, pp2 in ((0, pA), (1, pB)):
                            for j in range(8):
                                p.rec("tensor", lambda e, j=j, half=half, pp2=pp2: e.matmul(
                                    pp2[:, 0:512], gyfm[:, j * 128:(j + 1) * 128],
                                    WREG[4][:, j * 1024 + half * 512:j * 1024 + half * 512 + 512],
                                    start=(j == 0), stop=(j == 7)),
                                    waits=[tgyf, p.tok("ve")] if j == 0 else (), inc="pe")
                            tpe = p.tok("pe")
                            if DBGWO:
                                p.rec("vector", lambda e, xc=xc, half=half, pp2=pp2: e.tensor_copy(
                                    xc[:, half * 512:half * 512 + 512], pp2[:, 0:512]),
                                    waits=[tpe], inc="ve")
                            else:
                                p.rec("vector", lambda e, xc=xc, half=half, pp2=pp2: e.scalar_tensor_tensor(
                                    xc[:, half * 512:half * 512 + 512], pp2[:, 0:512], 1.0,
                                    xc[:, half * 512:half * 512 + 512], ALU.mult, ALU.add),
                                    waits=[tpe], inc="ve")
                    return epi

                pend = mk_epilogue()
            if pend is not None:
                pend()
                pend = None
            p.barrier()
            if stage.startswith("dbg") or lupto == "yr":
                return emit_out()

            load_weight_scaled(wpool[:, 0:16384], dram[f"w1{li}"],
                               lambda j: lncol[:, j:j + 1], 8, slab_w=2048)
            load_weight_cast(wpool[:, 16384:32768], dram[f"w2{li}"], 16)
            p.barrier()
            beta_row(wpool[:, 0:16384], 0, width=2048, slab_w=2048)
            p.barrier()
            ln_stats(xs[:, 0:1024], mv[:, 4:5], mv[:, 5:6])
            p.rec("vector", lambda e: e.tensor_scalar(
                xht[:], xs[:, 0:1024], mv[:, 4:5], mv[:, 5:6], ALU.mult, ALU.add),
                waits=[p.tok("pe")], inc="ve")
            tapply = p.tok("ve")
            tres_prev = p.tok("ve")
            for c in range(NCH):
                xc = xs[:, c * 1024:(c + 1) * 1024]
                transpose8(xht, xhf, extra_waits=[tapply])
                txhf = p.tok("ac")
                if c + 1 < NCH:
                    xn_ = xs[:, (c + 1) * 1024:(c + 2) * 1024]
                    ln_stats(xn_, mv[:, 4:5], mv[:, 5:6])
                    p.rec("vector", lambda e, xn_=xn_: e.tensor_scalar(
                        xht[:], xn_, mv[:, 4:5], mv[:, 5:6], ALU.mult, ALU.add),
                        waits=[txhf], inc="ve")
                    tapply = p.tok("ve")
                tok_gelu = {}
                for ft in range(16):
                    pp2 = pA if ft % 2 == 0 else pB
                    w = [txhf]
                    if ft >= 2:
                        w.append(tok_gelu[ft - 2])
                    b0 = 0 + ft * 128
                    p.rec("tensor", lambda e, b0=b0, pp2=pp2: e.matmul(
                        pp2[:, 0:128], rowbuf[0:1, b0:b0 + 128], onesb[0:1, 0:128],
                        start=True, stop=False), waits=w, inc="pe")
                    for j in range(8):
                        p.rec("tensor", lambda e, j=j, ft=ft, pp2=pp2: e.matmul(
                            pp2[:, 0:128], wpool[:, j * 2048 + ft * 128:j * 2048 + ft * 128 + 128],
                            xhf[:, j * 128:(j + 1) * 128], start=False, stop=(j == 7)), inc="pe")
                    tpe = p.tok("pe")
                    p.rec("scalar", lambda e, ft=ft, pp2=pp2: e.activation(
                        h1f[:, ft * 128:(ft + 1) * 128], pp2[:, 0:128], AF.Gelu_apprx_tanh),
                        waits=[tpe], inc="ac")
                    tok_gelu[ft] = p.tok("ac")
                for half in (0, 1):
                    for ft in range(16):
                        if ft == 0:
                            w = [tok_gelu[0 if half == 0 else 15], tres_prev]
                        else:
                            w = [tok_gelu[ft]] if half == 0 else ()
                        p.rec("tensor", lambda e, ft=ft, half=half: e.matmul(
                            pY[:, half * 512:half * 512 + 512],
                            h1f[:, ft * 128:(ft + 1) * 128],
                            wpool[:, 16384 + ft * 1024 + half * 512:16384 + ft * 1024 + half * 512 + 512],
                            start=(ft == 0), stop=(ft == 15)), waits=w, inc="pe")
                    tpe = p.tok("pe")
                    p.rec("vector", lambda e, xc=xc, half=half: e.scalar_tensor_tensor(
                        xc[:, half * 512:half * 512 + 512], pY[:, half * 512:half * 512 + 512],
                        1.0, xc[:, half * 512:half * 512 + 512], ALU.mult, ALU.add),
                        waits=[tpe], inc="ve")
                tres_prev = p.tok("ve")
            p.barrier()

        if stage.startswith("attnA"):
            return emit_out()

        for c in range(NCH):
            buf = ropeA if c % 2 == 0 else ropeB
            w = [p.tok("ve")]
            td = p.dma("sync", buf[:, 0:1024], x0[c * 128:(c + 1) * 128, :], waits=w,
                       sem=f"wl{c % 2}")
            p.rec("vector", lambda e, c=c, buf=buf: e.tensor_add(
                xs[:, c * 1024:(c + 1) * 1024], xs[:, c * 1024:(c + 1) * 1024], buf[:, 0:1024]),
                waits=[td], inc="ve")
        p.barrier()
        if stage == "res1":
            return emit_out()

        def emit_ffn(first):
            if first:
                p.dma("gpsimd", rowbuf[0:1, 3072:4096], dram["bf"][0:1, 0:1024])
                for j in range(8):
                    p.dma("sync", wf32[:, j * 1024:(j + 1) * 1024],
                          dram["wf"][j * 128:(j + 1) * 128, :])
                tw = p.tok("sd")
                p.rec("vector", lambda e: e.tensor_reduce(mv[:, 6:7], wf32[:, 0:8192],
                                                          AX.X, ALU.max,
                                                          apply_absolute_value=True),
                      waits=[tw], inc="ve")
                partition_amax_to_cell(mv[:, 6:7], sclr[0:1, 0:1])
                p.rec("vector", lambda e: e.tensor_single_scalar(sclr[0:1, 0:1], sclr[0:1, 0:1],
                                                                 1e-12, ALU.max), inc="ve")
                p.rec("vector", lambda e: e.reciprocal(sclr[0:1, 1:2], sclr[0:1, 0:1]), inc="ve")
                p.rec("vector", lambda e: e.tensor_single_scalar(sclr[0:1, 1:2], sclr[0:1, 1:2],
                                                                 448.0, ALU.mult), inc="ve")
                bcast_cell_to_col(sclr[0:1, 1:2], scl[:, 0:1])
                for j in range(8):
                    p.rec("vector", lambda e, j=j: e.tensor_copy(xnf[:, 0:1024],
                                                                 wf32[:, j * 1024:(j + 1) * 1024]), inc="ve")
                    quant(xnf[:, 0:1024], scl[:, 0:1], wq8[:, j * 1024:(j + 1) * 1024])
                p.barrier()
            for c in range(NCH):
                xc = xs[:, c * 1024:(c + 1) * 1024]
                ln_stats(xc, stashS[:, 2 * c:2 * c + 1], stashS[:, 2 * c + 1:2 * c + 2])
                p.rec("vector", lambda e, xc=xc, c=c: e.tensor_scalar(
                    xnf[:], xc, stashS[:, 2 * c:2 * c + 1], stashS[:, 2 * c + 1:2 * c + 2],
                    ALU.mult, ALU.add), inc="ve")
                p.rec("vector", lambda e, c=c: e.tensor_reduce(
                    amaxc[:, c:c + 1], xnf[:], AX.X, ALU.max, apply_absolute_value=True), inc="ve")
            p.rec("vector", lambda e: e.tensor_reduce(mv[:, 6:7], amaxc[:, 0:16], AX.X, ALU.max), inc="ve")
            partition_amax_to_cell(mv[:, 6:7], sclr[0:1, 2:3])
            tloc = p.tok("ve")
            tgd = p.dma("gpsimd", cc_in.ap(), sclr[0:1, 2:3], waits=[tloc], sem="ccd")
            p.rec("gpsimd", lambda e: e.collective_compute(
                "AllReduce", ALU.max, replica_groups=[list(range(8))],
                ins=[cc_in.ap()], outs=[cc_out.ap()]), waits=[tgd], inc="cc")
            tcc = p.tok("cc")
            p.rec("gpsimd", None, waits=[tcc])
            tgd = p.dma("gpsimd", sclr[0:1, 3:4], cc_out.ap(), sem="ccd")
            p.rec("vector", lambda e: e.tensor_single_scalar(sclr[0:1, 3:4], sclr[0:1, 3:4],
                                                             1e-12, ALU.max), waits=[tgd], inc="ve")
            p.rec("vector", lambda e: e.reciprocal(sclr[0:1, 4:5], sclr[0:1, 3:4]), inc="ve")
            p.rec("vector", lambda e: e.tensor_single_scalar(sclr[0:1, 4:5], sclr[0:1, 4:5],
                                                             448.0, ALU.mult), inc="ve")
            p.rec("vector", lambda e: e.tensor_mul(sclr[0:1, 5:6], sclr[0:1, 4:5],
                                                   sclr[0:1, 1:2]), inc="ve")
            p.rec("vector", lambda e: e.reciprocal(sclr[0:1, 6:7], sclr[0:1, 5:6]), inc="ve")
            bcast_cell_to_col(sclr[0:1, 4:5], scl[:, 1:2])
            bcast_cell_to_col(sclr[0:1, 6:7], scl[:, 2:3])
            p.rec("vector", lambda e: e.tensor_scalar_mul(
                rowbuf[0:1, 2048:3072], rowbuf[0:1, 3072:4096], sclr[0:1, 5:6]), inc="ve")
            p.barrier()
            for c in range(NCH):
                xc = xs[:, c * 1024:(c + 1) * 1024]
                p.rec("vector", lambda e, xc=xc, c=c: e.tensor_scalar(
                    xnf[:], xc, stashS[:, 2 * c:2 * c + 1], stashS[:, 2 * c + 1:2 * c + 2],
                    ALU.mult, ALU.add), waits=[p.tok("pe")], inc="ve")
                quant(xnf[:, 0:1024], scl[:, 1:2], xq8[:, 0:1024])
                tq8 = p.tok("ve")
                transpose8(xq8, xq8f, extra_waits=[tq8])
                txf = p.tok("ac")
                for half, pp2 in ((0, pA), (1, pB)):
                    b0 = 2048 + half * 512
                    p.rec("tensor", lambda e, b0=b0, pp2=pp2: e.matmul(
                        pp2[:, 0:512], onesb[0:1, :], rowbuf[0:1, b0:b0 + 512],
                        start=True, stop=False), waits=[txf, p.tok("ve")], inc="pe")
                    for j in range(8):
                        p.rec("tensor", lambda e, j=j, half=half, pp2=pp2: e.matmul(
                            pp2[:, 0:512], xq8f[:, j * 128:(j + 1) * 128],
                            wq8[:, j * 1024 + half * 512:j * 1024 + half * 512 + 512],
                            start=False, stop=(j == 7)), inc="pe")
                    tpe = p.tok("pe")
                    p.rec("vector", lambda e, xc=xc, half=half, pp2=pp2: e.scalar_tensor_tensor(
                        xc[:, half * 512:half * 512 + 512], pp2[:, 0:512], scl[:, 2:3],
                        xc[:, half * 512:half * 512 + 512], ALU.mult, ALU.add),
                        waits=[tpe], inc="ve")
            p.barrier()

        emit_ffn(True)
        if stage == "ffn1":
            return emit_out()

        t_r = p.dma("sync", rows8[0:8, 0:128], dram["bbq8"], waits=[p.tok("ve")], sem="rowd")
        prep_vec_cols(rows8[0:8, 0:128], bcolS[:, 0:8], tok_in=t_r)
        t_r = p.dma("sync", rows8[0:8, 0:128], dram["bbk8"], waits=[p.tok("ve")], sem="rowd")
        prep_vec_cols(rows8[0:8, 0:128], bcolS[:, 8:16], tok_in=t_r)
        p.dma("gpsimd", rowbuf[0:1, 0:1024], dram["bbv"][0:1, 0:1024])
        p.dma("gpsimd", rowbuf[0:1, 1024:2048], dram["bbo"][0:1, 0:1024])
        p.dma("sync", ones2S, dram["ones2"], waits=[p.tok("ve")], sem="rowd")
        p.dma("sync", sel2S, dram["sel2"], sem="rowd")
        t_r = p.dma("sync", sqscr[:, 0:128], dram["r2"], sem="rowd")
        p.rec("vector", lambda e: e.tensor_single_scalar(r2b[:], sqscr[:, 0:128], 0.125,
                                                         ALU.mult), waits=[t_r], inc="ve")
        p.rec("vector", lambda e: e.memset(kvacc[:, 0:512], 0.0), inc="ve")
        p.barrier()

        def fm_proj_featuremap(wreg, bcol_off, out_fm, c, tfm):
            for oc in range(8):
                tw0 = [tfm, p.tok("ve")]
                for j in range(8):
                    p.rec("tensor", lambda e, j=j, oc=oc, wreg=wreg: e.matmul(
                        pY[:, oc * 128:(oc + 1) * 128],
                        wreg[:, j * 1024 + oc * 128:j * 1024 + oc * 128 + 128],
                        xhf[:, j * 128:(j + 1) * 128],
                        start=(j == 0), stop=(j == 7)), waits=tw0 if j == 0 else (), inc="pe")
                tpe = p.tok("pe")
                p.rec("vector", lambda e, oc=oc, bo=bcol_off: e.tensor_scalar_add(
                    ropeA[:, oc * 128:(oc + 1) * 128], pY[:, oc * 128:(oc + 1) * 128],
                    bcolS[:, bo + oc:bo + oc + 1]), waits=[tpe], inc="ve")
            trw = p.tok("ve")
            p.rec("scalar", lambda e: e.copy(xht[:, 0:1024], ropeA[:, 0:1024]),
                  waits=[trw], inc="ac")
            p.rec("scalar", lambda e: e.activation(sqscr[:, 0:1024], ropeA[:, 0:1024],
                                                   AF.Square, scale=0.125), inc="ac")
            tsq = p.tok("ac")
            for oc in range(8):
                patt = pAt1 if oc % 2 == 0 else pAt2
                p.rec("tensor", lambda e, oc=oc, patt=patt: e.matmul(
                    patt[:, 0:128], r2b[:], xht[:, oc * 128:(oc + 1) * 128],
                    start=True, stop=True), waits=[tsq, p.tok("ve")], inc="pe")
                tproj = p.tok("pe")
                p.rec("tensor", lambda e, oc=oc: e.matmul(
                    pR[0:2, 0:128], ones2S[:, 0:2], sqscr[:, oc * 128:(oc + 1) * 128],
                    start=True, stop=True), inc="pe")
                tsn = p.tok("pe")
                p.rec("vector", lambda e: e.tensor_copy(rows8[0:2, 0:128], pR[0:2, 0:128]),
                      waits=[tsn], inc="ve")
                trows = p.tok("ve")
                p.rec("tensor", lambda e: e.matmul(pR[:, 128:256], sel2S[0:2, :],
                                                   rows8[0:2, 0:128], start=True, stop=True),
                      waits=[trows], inc="pe")
                tbc = p.tok("pe")
                p.rec("vector", lambda e, oc=oc: e.tensor_copy(
                    sqscr[:, oc * 128:(oc + 1) * 128], pR[:, 128:256]),
                    waits=[tbc], inc="ve")
                p.rec("vector", lambda e, oc=oc, patt=patt: e.scalar_tensor_tensor(
                    ropeB[:, oc * 128:(oc + 1) * 128], sqscr[:, oc * 128:(oc + 1) * 128],
                    -0.5, patt[:, 0:128], ALU.mult, ALU.add), waits=[tproj], inc="ve")
                tcomb = p.tok("ve")
                p.rec("scalar", lambda e, oc=oc, o=out_fm: e.activation(
                    o[:, oc * 128:(oc + 1) * 128], ropeB[:, oc * 128:(oc + 1) * 128],
                    AF.Exp), waits=[tcomb], inc="ac")
            return p.tok("ac")

        load_weight_cast(WREG[0], dram["wbk"], 8)
        load_weight_cast(WREG[1], dram["wbv"], 8)
        p.barrier()
        for c in range(NCH):
            xc = xs[:, c * 1024:(c + 1) * 1024]
            ln_stats(xc, mv[:, 4:5], mv[:, 5:6])
            p.rec("vector", lambda e, xc=xc: e.tensor_scalar(
                xht[:], xc, mv[:, 4:5], mv[:, 5:6], ALU.mult, ALU.add),
                waits=[p.tok("pe")], inc="ve")
            tapply = p.tok("ve")
            transpose8(xht, xhf, extra_waits=[tapply])
            tfm = p.tok("ac")
            tkp = fm_proj_featuremap(WREG[0], 8, k2f, c, tfm)
            transpose8(k2f, ktm2, extra_waits=[tkp])
            tktm = p.tok("ac")
            xf_tiles = lambda j: xhf[:, j * 128:(j + 1) * 128]
            tm_proj(pY[:, 0:512], 0, WREG[1], xf_tiles, 0, extra_waits=[tktm])
            tvp = tm_proj(pY[:, 512:1024], 1, WREG[1], xf_tiles, 0)
            p.rec("scalar", lambda e: e.copy(vtm[:, 0:1024], pY[:, 0:1024]),
                  waits=[tvp], inc="ac")
            tvc = p.tok("ac")
            for h in range(NHB):
                pc, odd = h // 2, h % 2
                tp = (0, 64) if odd else None
                tw = [tktm, tvc]
                if h < 2:
                    tw.append(p.tok("ve"))
                p.rec("tensor", lambda e, h=h, pc=pc, odd=odd, tp=tp: e.matmul(
                    pA[odd * 64:odd * 64 + 64, pc * 64:pc * 64 + 64],
                    ktm2[:, h * 64:(h + 1) * 64], vtm[:, h * 64:(h + 1) * 64],
                    start=True, stop=True, tile_position=tp),
                    waits=tw if h < 2 else (), inc="pe")
            tkv = p.tok("pe")
            p.rec("vector", lambda e: e.tensor_add(kvacc[:, 0:512], kvacc[:, 0:512],
                                                   pA[:, 0:512]), waits=[tkv], inc="ve")
        p.barrier()
        p.rec("vector", lambda e: e.memset(kvq[:, 0:1056], 0.0), inc="ve")
        kvv = kvq[:, 0:1056].rearrange("p (pc w) -> p pc w", pc=8)
        kacv = kvacc[:, 0:512].rearrange("p (pc w) -> p pc w", pc=8)
        p.rec("vector", lambda e: e.tensor_copy(kvv[0:64, :, 0:64], kacv[0:64, :, :]), inc="ve")
        p.rec("vector", lambda e: e.tensor_copy(kvv[64:128, :, 66:130], kacv[64:128, :, :]), inc="ve")
        for pc in range(8):
            p.rec("vector", lambda e, pc=pc: e.memset(kvq[0:64, pc * 132 + 64:pc * 132 + 65], 1.0), inc="ve")
            p.rec("vector", lambda e, pc=pc: e.memset(kvq[64:128, pc * 132 + 130:pc * 132 + 131], 1.0), inc="ve")
        load_weight_cast(WREG[3], dram["wbq"], 8)
        load_weight_cast(WREG[4], dram["wbo"], 8)
        p.barrier()
        for c in range(NCH):
            xc = xs[:, c * 1024:(c + 1) * 1024]
            ln_stats(xc, mv[:, 4:5], mv[:, 5:6])
            p.rec("vector", lambda e, xc=xc: e.tensor_scalar(
                xht[:], xc, mv[:, 4:5], mv[:, 5:6], ALU.mult, ALU.add),
                waits=[p.tok("pe")], inc="ve")
            tapply = p.tok("ve")
            transpose8(xht, xhf, extra_waits=[tapply])
            tfm = p.tok("ac")
            tqp = fm_proj_featuremap(WREG[3], 0, qf, c, tfm)
            for rnd in range(2):
                for s_ in range(4):
                    pc = rnd * 4 + s_
                    w = [tqp, p.tok("ve")] if s_ == 0 else ()
                    p.rec("tensor", lambda e, pc=pc, s_=s_: e.matmul(
                        pY[:, s_ * 256:s_ * 256 + 131], qf[:, pc * 128:(pc + 1) * 128],
                        kvq[:, pc * 132:pc * 132 + 131], start=True, stop=True),
                        waits=w, inc="pe")
                trnd = p.tok("pe")
                pYr = pY[:, 0:1024].rearrange("p (s w) -> p s w", s=4)
                p.rec("vector", lambda e: e.tensor_single_scalar(
                    gnm[:, 0:4].unsqueeze(2), pYr[:, :, 64:65], 1e-8, ALU.add),
                    waits=[trnd], inc="ve")
                p.rec("vector", lambda e: e.tensor_single_scalar(
                    gnm[:, 4:8].unsqueeze(2), pYr[:, :, 130:131], 1e-8, ALU.add), inc="ve")
                p.rec("vector", lambda e: e.reciprocal(gnm[:, 0:8], gnm[:, 0:8]), inc="ve")
                att_ev = ktm2[:, rnd * 512:(rnd + 1) * 512].rearrange("p (s w) -> p s w", s=4)
                zE = gnm[:, 0:4].unsqueeze(2).broadcast_to((128, 4, 64))
                zO = gnm[:, 4:8].unsqueeze(2).broadcast_to((128, 4, 64))
                p.rec("vector", lambda e, a=att_ev, z=zE: e.tensor_tensor(
                    a[:, :, 0:64], pYr[:, :, 0:64], z, ALU.mult), inc="ve")
                p.rec("vector", lambda e, a=att_ev, z=zO: e.tensor_tensor(
                    a[:, :, 64:128], pYr[:, :, 66:130], z, ALU.mult), inc="ve")
            tato = p.tok("ve")
            transpose8(ktm2, gyfm, extra_waits=[tato])
            tof = p.tok("ac")
            for half, pp2 in ((0, pA), (1, pB)):
                b0 = 1024 + half * 512
                p.rec("tensor", lambda e, b0=b0, pp2=pp2: e.matmul(
                    pp2[:, 0:512], onesb[0:1, :], rowbuf[0:1, b0:b0 + 512],
                    start=True, stop=False), waits=[tof, p.tok("ve")], inc="pe")
                for j in range(8):
                    p.rec("tensor", lambda e, j=j, half=half, pp2=pp2: e.matmul(
                        pp2[:, 0:512], gyfm[:, j * 128:(j + 1) * 128],
                        WREG[4][:, j * 1024 + half * 512:j * 1024 + half * 512 + 512],
                        start=False, stop=(j == 7)), inc="pe")
                tpe = p.tok("pe")
                p.rec("vector", lambda e, xc=xc, half=half, pp2=pp2: e.scalar_tensor_tensor(
                    xc[:, half * 512:half * 512 + 512], pp2[:, 0:512], 1.0,
                    xc[:, half * 512:half * 512 + 512], ALU.mult, ALU.add),
                    waits=[tpe], inc="ve")
        p.barrier()
        if stage == "attnB":
            return emit_out()

        emit_ffn(False)
        return emit_out()
    return nc


def kernel(**inputs):
    X = np.ascontiguousarray(inputs["L0_attn_a_X"], dtype=np.float32)
    t = _host_tables()
    common = {k: v for k, v in t.items() if k != "gc128"}
    for i in range(L):
        common[f"wq{i}"] = _perm_cols(inputs["L0_attn_a_Wq"][i])
        common[f"wk{i}"] = _perm_cols(inputs["L0_attn_a_Wk"][i])
        for nm, key in [("wv", "L0_attn_a_Wv"), ("wg", "L0_attn_a_Wg"),
                        ("wo", "L0_attn_a_Wo"), ("w1", "L0_attn_a_W1"),
                        ("w2", "L0_attn_a_W2")]:
            common[f"{nm}{i}"] = np.ascontiguousarray(inputs[key][i], dtype=np.float32)
    common["ln_s"] = np.ascontiguousarray(np.asarray(inputs["L0_attn_a_ln_scale"]).reshape(8, 128), dtype=np.float32)
    common["ln_b"] = np.ascontiguousarray(np.asarray(inputs["L0_attn_a_ln_bias"]).reshape(8, 128), dtype=np.float32)
    common["gn_s"] = np.ascontiguousarray(np.asarray(inputs["L0_attn_a_gn_scale"]).reshape(L, 8, 128), dtype=np.float32)
    common["gn_b"] = np.ascontiguousarray(np.asarray(inputs["L0_attn_a_gn_bias"]).reshape(L, 8, 128), dtype=np.float32)
    for nm, key in [("wbq", "L0_attn_b_Wq"), ("wbk", "L0_attn_b_Wk"),
                    ("wbv", "L0_attn_b_Wv"), ("wbo", "L0_attn_b_Wo")]:
        common[nm] = np.ascontiguousarray(np.asarray(inputs[key]).T, dtype=np.float32)
    for nm, key in [("bbv", "L0_attn_b_bv"), ("bbo", "L0_attn_b_bo"), ("bf", "L0_ffn_bias")]:
        common[nm] = np.ascontiguousarray(np.asarray(inputs[key]).reshape(1, D), dtype=np.float32)
    common["bbq8"] = np.ascontiguousarray(np.asarray(inputs["L0_attn_b_bq"]).reshape(8, 128), dtype=np.float32)
    common["bbk8"] = np.ascontiguousarray(np.asarray(inputs["L0_attn_b_bk"]).reshape(8, 128), dtype=np.float32)
    R = np.asarray(inputs["L0_attn_b_R"], dtype=np.float32)
    R2 = np.zeros((128, 128), dtype=np.float32)
    R2[0:64, 0:64] = R
    R2[64:128, 64:128] = R
    common["r2"] = R2
    common["wf"] = np.ascontiguousarray(np.asarray(inputs["L0_ffn_weight"]).T, dtype=np.float32)

    nc = build_program(_STAGE, t)
    in_maps = []
    for b in range(B):
        m = dict(common)
        m["x0"] = np.ascontiguousarray(X[b])
        in_maps.append(m)
    res = run_bass_kernel_spmd(nc, in_maps, core_ids=list(range(B)))
    return np.stack([np.asarray(res.results[b]["out"]) for b in range(B)], axis=0)


# version 08 (final)
# speedup vs baseline: 1.0247x; 1.0247x over previous
import numpy as np
from contextlib import ExitStack
import concourse.bass as bass
import concourse.mybir as mybir
from concourse.bass_utils import run_bass_kernel_spmd

F32 = mybir.dt.float32
BF16 = mybir.dt.bfloat16
I32 = mybir.dt.int32
AF = mybir.ActivationFunctionType
ALU = mybir.AluOpType
AX = mybir.AxisListType

B, S, D, H, DH, FF, L = 8, 2048, 1024, 8, 128, 2048, 4
NCH = S // 128
NHB = 16
EPS = 1e-5

_STAGE = "full"
_SIM_BLEND = False

ENGINES = ("sync", "gpsimd", "vector", "scalar", "tensor")
_OWN = {"sync": "sd", "gpsimd": "gd", "vector": "ve", "scalar": "ac", "tensor": "pe"}


class Prog:
    def __init__(self, nc):
        self.nc = nc
        self.q = {e: [] for e in ENGINES}
        self.cnt = {}
        self.sems = {}

    def _ensure(self, s):
        if s not in self.cnt:
            self.cnt[s] = 0

    def rec(self, eng, fn, waits=(), inc=None, inc_n=1):
        if inc is not None:
            self._ensure(inc)
            self.cnt[inc] += inc_n
        for (s, v) in waits:
            self._ensure(s)
        self.q[eng].append({"fn": fn, "waits": [w for w in waits if w[1] > 0],
                            "inc": inc, "inc_n": inc_n})
        return (inc, self.cnt[inc]) if inc else None

    def tok(self, s):
        self._ensure(s)
        return (s, self.cnt[s])

    def dma(self, eng, out, in_, waits=(), sem=None, transpose=False):
        s = sem or ("sd" if eng == "sync" else "gd")
        if transpose:
            fn = lambda e, out=out, in_=in_: e.dma_start_transpose(out=out, in_=in_)
        else:
            fn = lambda e, out=out, in_=in_: e.dma_start(out=out, in_=in_)
        return self.rec(eng, fn, waits=waits, inc=s, inc_n=16)

    def barrier(self):
        toks = [self.tok(s) for s in self.cnt]
        for e in ENGINES:
            own = _OWN[e]
            w = [t for t in toks if t[0] != own and t[1] > 0]
            if w:
                self.rec(e, None, waits=w)

    def emit(self, block):
        def make(eng_name):
            drain_eng = eng_name in ("vector", "scalar")
            def run(eng):
                for r in self.q[eng_name]:
                    for (s, v) in r["waits"]:
                        eng.wait_ge(self.sems[s], v)
                    if r["fn"] is None:
                        continue
                    inst = r["fn"](eng)
                    if r["inc"] is not None:
                        if drain_eng and r["inc_n"] == 1:
                            eng.drain().then_inc(self.sems[r["inc"]], r["inc_n"])
                        else:
                            inst.then_inc(self.sems[r["inc"]], r["inc_n"])
            return run
        block.sync(make("sync"))
        block.gpsimd(make("gpsimd"))
        block.vector(make("vector"))
        block.scalar(make("scalar"))
        block.tensor(make("tensor"))


def _host_tables():
    t = {}
    pos = np.arange(S, dtype=np.float32)
    expo = np.arange(0, DH, 2).astype(np.float32) / np.float32(DH)
    inv_freq = (np.float32(1.0) / np.power(np.float32(10000.0), expo,
                                           dtype=np.float32)).astype(np.float32)
    ang = (pos[:, None] * inv_freq[None, :]).astype(np.float32).astype(np.float64)

    def to_chunks(a):
        a = a.reshape(NCH, 128, DH // 2).transpose(1, 0, 2).reshape(128, NCH * (DH // 2))
        return np.ascontiguousarray(a.astype(np.float32))
    t["cos_tab"] = to_chunks(np.cos(ang))
    t["sin_tab"] = to_chunks(np.sin(ang))
    lin = np.linspace(np.log(np.float64(1 / 32)), np.log(np.float64(1 / 512)), H)
    gammas32 = (np.float32(1.0) - np.exp(lin.astype(np.float32))).astype(np.float32)
    g = gammas32.astype(np.float64)
    pp = np.arange(128, dtype=np.float64)
    t["gamq"] = np.ascontiguousarray((g[None, :] ** (pp[:, None] + 1)).astype(np.float32))
    t["gamk2"] = np.ascontiguousarray((g[None, :] ** (-(pp[:, None] + 1))).astype(np.float32))
    t["gk128"] = np.ascontiguousarray(
        np.broadcast_to((g ** 128).astype(np.float32)[None, :], (128, H)).copy())
    t["gc128"] = [float(v) for v in (g ** 128).astype(np.float32)]
    m = np.arange(128)
    t["maskT"] = np.ascontiguousarray((m[None, :] >= m[:, None]).astype(np.float32))
    t["identf"] = np.eye(128, dtype=np.float32)
    ones2 = np.zeros((128, 2), dtype=np.float32)
    ones2[0:64, 0] = 1.0
    ones2[64:128, 1] = 1.0
    t["ones2"] = ones2
    sel2 = np.zeros((2, 128), dtype=np.float32)
    sel2[0, 0:64] = 1.0
    sel2[1, 64:128] = 1.0
    t["sel2"] = sel2
    return t


def _perm_cols(w):
    w = np.asarray(w)
    perm = np.concatenate([np.arange(0, 128, 2), np.arange(1, 128, 2)])
    idx = np.concatenate([h * 128 + perm for h in range(H)])
    return np.ascontiguousarray(w[:, idx].astype(np.float32))


def build_program(stage, tabs):
    nc = bass.Bass("TRN2", target_bir_lowering=False)
    dram = {}

    def din(name, shape):
        dram[name] = nc.dram_tensor(name, list(shape), F32, kind="ExternalInput").ap()
        return dram[name]

    x0 = din("x0", (S, D))
    for i in range(L):
        for nm, sh in [("wq", (D, D)), ("wk", (D, D)), ("wv", (D, D)), ("wg", (D, D)),
                       ("wo", (D, D)), ("w1", (D, FF)), ("w2", (FF, D))]:
            din(f"{nm}{i}", sh)
    din("ln_s", (8, 128)); din("ln_b", (8, 128))
    din("gn_s", (L, 8, 128)); din("gn_b", (L, 8, 128))
    din("cos_tab", (128, NCH * 64)); din("sin_tab", (128, NCH * 64))
    din("gamq", (128, H)); din("gamk2", (128, H)); din("gk128", (128, H))
    din("maskT", (128, 128)); din("identf", (128, 128))
    din("ones2", (128, 2)); din("sel2", (2, 128))
    din("wbq", (D, D)); din("wbk", (D, D)); din("wbv", (D, D)); din("wbo", (D, D))
    din("bbv", (1, D)); din("bbo", (1, D)); din("bf", (1, D))
    din("bbq8", (8, 128)); din("bbk8", (8, 128))
    din("r2", (128, 128))
    din("wf", (D, D))
    out = nc.dram_tensor("out", [S, D], F32, kind="ExternalOutput").ap()
    cc_in = nc.dram_tensor("cc_in", [1, 1], F32, kind="Internal")
    cc_out = nc.dram_tensor("cc_out", [1, 1], F32, kind="Internal")

    es = ExitStack()
    with es:
        def sbuf(name, shape, dt=F32):
            return es.enter_context(nc.sbuf_tensor(name, list(shape), dt))

        xs = sbuf("xs", (128, NCH * 1024))
        wpool = sbuf("wpool", (128, 5 * 8192), BF16)
        BP = sbuf("BP", (128, 9472), BF16)
        ropeA = sbuf("ropeA", (128, 1024))
        ropeB = sbuf("ropeB", (128, 1024))
        sqscr = sbuf("sqscr", (128, 1024))
        xnf = sbuf("xnf", (128, 1024))
        QI2 = sbuf("QI2", (128, 1024), I32)
        MKI = sbuf("MKI", (128, 1024), I32)
        tabq = sbuf("tabq", (128, 1024), BF16)
        Sst = sbuf("Sst", (128, 1024))
        S16 = sbuf("S16", (128, 1024), BF16)
        gnbg = sbuf("gnbg", (128, 1024), BF16)
        rowbuf = sbuf("rowbuf", (1, 4096), BF16)
        gamqS = sbuf("gamqS", (128, H))
        gamk2S = sbuf("gamk2S", (128, H))
        gk128S = sbuf("gk128S", (128, H))
        maskS = sbuf("maskS", (128, 128), BF16)
        idfS = sbuf("idfS", (128, 128))
        idbS = sbuf("idbS", (128, 128), BF16)

        lncol = sbuf("lncol", (128, 24))
        gncol = sbuf("gncol", (128, 8))
        statsS = sbuf("statsS", (128, 16))
        mv = sbuf("mv", (128, 8))
        gnm = sbuf("gnm", (128, 32))


        onesf = sbuf("onesf", (1, 128))
        onesb = sbuf("onesb", (1, 128), BF16)
        bcol16 = sbuf("bcol16", (128, 8), BF16)
        scl = sbuf("scl", (128, 4))
        sclr = sbuf("sclr", (1, 8))



        WREG = [wpool[:, r * 8192:(r + 1) * 8192] for r in range(5)]
        wq8 = WREG[2]
        wf32 = wpool[:, 0:16384].bitcast(F32)
        xht = BP[:, 0:1024]
        xhf = BP[:, 1024:2048]
        qtm = BP[:, 2048:3072]
        ktm2 = BP[:, 3072:4096]
        ktms = BP[:, 4096:5120]
        vtm = BP[:, 5120:6144]
        gtm = BP[:, 6144:7168]
        qf = BP[:, 7168:8192]
        k2f = BP[:, 8192:9216]
        attm2 = BP[:, 9216:9472]
        h1f = BP[:, 2048:4096]
        kvq = BP[:, 4096:5152]
        xq8 = qtm
        xq8f = qf
        gyfm = k2f
        gybuf = QI2[:, 0:1024].bitcast(BF16)[:, 0:1024]
        kvacc = Sst[:, 0:512]
        rows8 = xnf
        tabk = tabq
        r2b = BP[:, 9216:9344]
        stashS = Sst[:, 642:674]
        amaxc = gnm[:, 0:16]
        bcolS = lncol[:, 0:16]
        sel2S = Sst[0:2, 512:640]
        ones2S = Sst[:, 640:642]
        coslot = MKI.bitcast(F32)[:, 0:64]
        sinlot = MKI.bitcast(F32)[:, 64:128]

        pdef = lambda name, shape, dt=F32: es.enter_context(nc.psum_tensor(name, list(shape), dt))
        pA = pdef("pA", (128, 512))
        pB = pdef("pB", (128, 512))
        pY = pdef("pY", (128, 1024))
        pT = pdef("pT", (128, 1024), BF16)
        pAt1 = pdef("pAt1", (128, 512))
        pAt2 = pdef("pAt2", (128, 512))
        pR = pdef("pR", (128, 512))

        p = Prog(nc)

        for name, dst in [("gamq", gamqS), ("gamk2", gamk2S), ("gk128", gk128S),
                          ("identf", idfS)]:
            p.dma("sync", dst[:], dram[name])
        p.dma("gpsimd", idbS[:], dram["identf"])
        p.dma("gpsimd", maskS[:], dram["maskT"])
        for c in range(NCH):
            p.dma("sync", xs[:, c * 1024:(c + 1) * 1024], x0[c * 128:(c + 1) * 128, :])
        p.rec("vector", lambda e: e.memset(onesf[:], 1.0), inc="ve")
        p.rec("vector", lambda e: e.memset(onesb[:], 1.0), inc="ve")

        def prep_vec_cols(rows_ap, out_cols_ap, n=8, tok_in=None):
            tw = [p.tok("ve"), p.tok("ac")]
            if tok_in is not None:
                tw.append(tok_in)
            p.rec("tensor", lambda e, r=rows_ap, n=n: e.transpose(
                pR[:, 0:n], r, idfS[0:n, 0:n]), waits=tw, inc="pe")
            tpe = p.tok("pe")
            p.rec("vector", lambda e, o=out_cols_ap, n=n: e.tensor_copy(o, pR[:, 0:n]),
                  waits=[tpe], inc="ve")
            return p.tok("ve")

        p.barrier()
        t_r = p.dma("sync", rows8[0:8, 0:128], dram["ln_s"], sem="rowd")
        prep_vec_cols(rows8[0:8, 0:128], lncol[:, 0:8], tok_in=t_r)
        t_r = p.dma("sync", rows8[0:8, 0:128], dram["ln_b"], waits=[p.tok("ve")], sem="rowd")
        prep_vec_cols(rows8[0:8, 0:128], lncol[:, 8:16], tok_in=t_r)
        p.rec("vector", lambda e: e.reciprocal(lncol[:, 16:24], lncol[:, 0:8]), inc="ve")
        p.rec("vector", lambda e: e.tensor_mul(lncol[:, 16:24], lncol[:, 16:24],
                                               lncol[:, 8:16]), inc="ve")
        p.rec("vector", lambda e: e.tensor_copy(bcol16[:, 0:8], lncol[:, 16:24]), inc="ve")
        p.barrier()

        def ln_stats(src_ap, rstd_col, negmur_col):
            p.rec("vector", lambda e, s=src_ap: e.bn_stats(statsS[:, 0:6], s[:, 0:512]), inc="ve")
            p.rec("vector", lambda e, s=src_ap: e.bn_stats(statsS[:, 6:12], s[:, 512:1024]), inc="ve")
            p.rec("vector", lambda e: e.bn_aggr(mv[:, 0:2], statsS[:, 0:12]), inc="ve")
            p.rec("vector", lambda e: e.tensor_single_scalar(mv[:, 2:3], mv[:, 1:2], EPS, ALU.add), inc="ve")
            tva = p.tok("ve")
            p.rec("scalar", lambda e: e.activation(mv[:, 3:4], mv[:, 2:3], AF.Sqrt),
                  waits=[tva], inc="ac")
            tac = p.tok("ac")
            p.rec("vector", lambda e, r=rstd_col: e.reciprocal(r, mv[:, 3:4]), waits=[tac], inc="ve")
            p.rec("vector", lambda e, r=rstd_col, nm2=negmur_col: e.tensor_scalar(
                nm2, mv[:, 0:1], r, -1.0, ALU.mult, ALU.mult), inc="ve")
            return p.tok("ve")

        def transpose8(src_sb, dst_sb, src_cols=None, dst_cols=None, n=8, extra_waits=()):
            tw = list(extra_waits) + [p.tok("ac")]
            for j in range(n):
                sc = src_cols(j) if src_cols else (j * 128, (j + 1) * 128)
                p.rec("tensor", lambda e, sc=sc, j=j, s=src_sb: e.transpose(
                    pT[:, j * 128:(j + 1) * 128], s[:, sc[0]:sc[1]], idbS[:]),
                    waits=tw if j == 0 else (), inc="pe")
            tpe = p.tok("pe")
            for j in range(n):
                dc = dst_cols(j) if dst_cols else (j * 128, (j + 1) * 128)
                p.rec("scalar", lambda e, dc=dc, j=j, d=dst_sb: e.copy(
                    d[:, dc[0]:dc[1]], pT[:, j * 128:(j + 1) * 128]),
                    waits=[tpe] if j == 0 else (), inc="ac")
            return p.tok("ac")

        def load_weight_scaled(dst_region, src_dram, scale_col_fn, nslab, slab_w=1024):
            n_pieces = nslab * (slab_w // 1024)
            for pc in range(n_pieces):
                j = pc // (slab_w // 1024)
                off = (pc % (slab_w // 1024)) * 1024
                buf = ropeA if pc % 2 == 0 else ropeB
                w = [p.tok("ve")]
                td = p.dma("sync", buf[:, 0:1024],
                           src_dram[j * 128:(j + 1) * 128, off:off + 1024], waits=w,
                           sem=f"wl{pc % 2}")
                sc = scale_col_fn(j)
                p.rec("vector", lambda e, buf=buf, j=j, off=off, sc=sc, d=dst_region, sw=slab_w:
                      e.tensor_scalar_mul(d[:, j * sw + off:j * sw + off + 1024], buf[:, 0:1024], sc),
                      waits=[td], inc="ve")
            return p.tok("ve")

        def load_weight_cast(dst_region, src_dram, nslab, slab_w=1024):
            tok = None
            for j in range(nslab):
                tok = p.dma("gpsimd", dst_region[:, j * slab_w:(j + 1) * slab_w],
                            src_dram[j * 128:(j + 1) * 128, 0:slab_w])
            return tok

        def beta_row(weight_region, out_col0, width=1024, nchunk=8, slab_w=1024):
            for s_ in range((width + 511) // 512):
                w0, w1 = s_ * 512, min(width, s_ * 512 + 512)
                for j in range(nchunk):
                    wts = [p.tok("ve")] if j == 0 else ()
                    p.rec("tensor", lambda e, j=j, w0=w0, w1=w1, wr=weight_region, sw=slab_w:
                          e.matmul(pR[0:1, 0:w1 - w0], bcol16[:, j:j + 1],
                                   wr[:, j * sw + w0:j * sw + w1],
                                   start=(j == 0), stop=(j == nchunk - 1)), waits=wts, inc="pe")
                tpe = p.tok("pe")
                p.rec("vector", lambda e, w0=w0, w1=w1, o=out_col0: e.tensor_copy(
                    rowbuf[0:1, o + w0:o + w1], pR[0:1, 0:w1 - w0]),
                    waits=[tpe], inc="ve")
            return p.tok("ve")

        def tm_proj(psum_half, half, wregion, xf_tiles, bias_row_col0, extra_waits=(),
                    auto=True):
            ew = list(extra_waits) + ([p.tok("ve"), p.tok("ac")] if auto else [])
            if bias_row_col0 is not None:
                b0 = bias_row_col0 + half * 512
                p.rec("tensor", lambda e, b0=b0, ph=psum_half: e.matmul(
                    ph, onesb[0:1, :], rowbuf[0:1, b0:b0 + 512],
                    start=True, stop=False), waits=ew, inc="pe")
                first = False
            else:
                first = True
            for j in range(8):
                st = first and (j == 0)
                p.rec("tensor", lambda e, j=j, st=st, ph=psum_half, wr=wregion, xf=xf_tiles, half=half:
                      e.matmul(ph, xf(j), wr[:, j * 1024 + half * 512:j * 1024 + half * 512 + 512],
                               start=st, stop=(j == 7)),
                      waits=ew if st else (), inc="pe")
            return p.tok("pe")

        def quant(src_f32, scale_col, out_bf16, width=1024):
            QI1v = sqscr.bitcast(I32)
            p.rec("vector", lambda e, s=src_f32, sc=scale_col: e.tensor_scalar_mul(s, s, sc), inc="ve")
            u = src_f32.bitcast(I32)
            p.rec("vector", lambda e, u=u: e.tensor_single_scalar(QI1v[:, 0:width], u, 20,
                                                                  ALU.logical_shift_right), inc="ve")
            p.rec("vector", lambda e, u=u: e.tensor_single_scalar(QI2[:, 0:width], u, 0xFFFFF,
                                                                  ALU.bitwise_and), inc="ve")
            p.rec("vector", lambda e: e.tensor_single_scalar(MKI[:, 0:width], QI1v[:, 0:width],
                                                             1, ALU.bitwise_and), inc="ve")
            p.rec("vector", lambda e: e.tensor_add(QI2[:, 0:width], MKI[:, 0:width],
                                                   QI2[:, 0:width]), inc="ve")
            p.rec("vector", lambda e: e.scalar_tensor_tensor(QI2[:, 0:width], QI2[:, 0:width],
                                                             524289, QI1v[:, 0:width],
                                                             ALU.is_ge, ALU.add), inc="ve")
            p.rec("vector", lambda e: e.tensor_single_scalar(ropeA[:, 0:width].bitcast(I32),
                                                             QI2[:, 0:width], 20,
                                                             ALU.logical_shift_left), inc="ve")
            p.rec("vector", lambda e: e.tensor_single_scalar(MKI[:, 0:width], QI1v[:, 0:width],
                                                             0x7FF, ALU.bitwise_and), inc="ve")
            p.rec("vector", lambda e: e.tensor_single_scalar(MKI[:, 0:width], MKI[:, 0:width],
                                                             968, ALU.is_ge), inc="ve")
            p.rec("vector", lambda e, s=src_f32: e.tensor_scalar(ropeB[:, 0:width], s,
                                                                 24576.0, -24576.0,
                                                                 ALU.add, ALU.add), inc="ve")
            if _SIM_BLEND:
                p.rec("vector", lambda e: e.tensor_copy(sqscr[:, 0:width], MKI[:, 0:width]), inc="ve")
                p.rec("vector", lambda e: e.tensor_tensor(ropeA[:, 0:width], ropeA[:, 0:width],
                                                          ropeB[:, 0:width], ALU.subtract), inc="ve")
                p.rec("vector", lambda e: e.tensor_tensor(ropeA[:, 0:width], ropeA[:, 0:width],
                                                          sqscr[:, 0:width], ALU.mult), inc="ve")
                p.rec("vector", lambda e, o=out_bf16: e.tensor_tensor(o, ropeA[:, 0:width],
                                                                      ropeB[:, 0:width], ALU.add), inc="ve")
            else:
                p.rec("vector", lambda e, o=out_bf16: e.select(o, MKI[:, 0:width], ropeA[:, 0:width],
                                                               ropeB[:, 0:width]), inc="ve")
            return p.tok("ve")

        def partition_amax_to_cell(col_ap, out_cell):
            tw = [p.tok("ve")]
            p.rec("tensor", lambda e, c=col_ap: e.transpose(pR[0:1, 0:128], c, idfS[:]),
                  waits=tw, inc="pe")
            tpe = p.tok("pe")
            p.rec("vector", lambda e, o=out_cell: e.tensor_reduce(o, pR[0:1, 0:128], AX.X, ALU.max),
                  waits=[tpe], inc="ve")
            return p.tok("ve")

        def bcast_cell_to_col(cell_ap, col_out):
            tw = [p.tok("ve")]
            p.rec("tensor", lambda e, c=cell_ap: e.matmul(pR[:, 0:1], onesf[0:1, :], c,
                                                          start=True, stop=True), waits=tw, inc="pe")
            tpe = p.tok("pe")
            p.rec("vector", lambda e, o=col_out: e.tensor_copy(o, pR[:, 0:1]), waits=[tpe], inc="ve")
            return p.tok("ve")

        def emit_out():
            p.barrier()
            for c in range(NCH):
                p.dma("sync", out[c * 128:(c + 1) * 128, :], xs[:, c * 1024:(c + 1) * 1024])
            p.rec("sync", None, waits=[p.tok("sd")])
            for s in list(p.cnt):
                p.sems[s] = es.enter_context(nc.semaphore(s))
            nums = sorted(h.num for h in p.sems.values())
            srange = range(nums[0], nums[-1] + 1)
            nc.all_engine_barrier()
            nc.gpsimd.dma_reset(srange)
            nc.gpsimd.sem_clear(srange)
            nc.all_engine_barrier()
            p.emit(es.enter_context(nc.Block()))
            return nc

        for c in range(NCH):
            xc = xs[:, c * 1024:(c + 1) * 1024]
            ln_stats(xc, mv[:, 4:5], mv[:, 5:6])
            if stage.startswith("ln1dbg"):
                p.rec("vector", lambda e, c=c: e.tensor_copy(stashS[:, 2*c:2*c+2], mv[:, 4:6]), inc="ve")
            if stage != "ln1dbg2":
                p.rec("vector", lambda e, xc=xc: e.tensor_scalar(
                    xc, xc, mv[:, 4:5], mv[:, 5:6], ALU.mult, ALU.add), inc="ve")
        p.barrier()
        if stage.startswith("ln1dbg"):
            p.rec("vector", lambda e: e.tensor_copy(ropeA[:, 0:32], stashS[:, 0:32]), inc="ve")
            p.barrier()
            p.rec("vector", lambda e: e.tensor_copy(xs[:, 0:32], ropeA[:, 0:32]), inc="ve")
        if stage in ("ln1", "ln1dbg", "ln1dbg2"):
            return emit_out()

        n_layers, upto = L, "full"
        if stage.startswith("attnA"):
            _, k_, upto = stage.split(":")
            n_layers = int(k_)
        if stage.startswith("dbg"):
            n_layers = int(stage[-1]) if stage[-1].isdigit() else 1

        for li in range(n_layers):
            lupto = upto if (li == n_layers - 1) else "full"
            if stage.rstrip("0123456789") == "dbgx" and li == n_layers - 1:
                return emit_out()
            for wi, nm in enumerate(["wq", "wk", "wv", "wg"]):
                load_weight_scaled(WREG[wi], dram[f"{nm}{li}"], lambda j: lncol[:, j:j + 1], 8)
            p.barrier()
            for wi in range(4):
                beta_row(WREG[wi], wi * 1024)
            t_r = p.dma("sync", rows8[0:8, 0:128], dram["gn_s"][li], waits=[p.tok("ve")],
                        sem="rowd")
            prep_vec_cols(rows8[0:8, 0:128], gncol[:, 0:8], tok_in=t_r)
            t_r2 = p.dma("sync", rows8[0:8, 128:256], dram["gn_b"][li],
                         waits=[p.tok("ve")], sem="rowd")
            p.rec("vector", lambda e: e.reciprocal(rows8[0:8, 0:128], rows8[0:8, 0:128]),
                  waits=[t_r2], inc="ve")
            p.rec("vector", lambda e: e.tensor_mul(rows8[0:8, 0:128], rows8[0:8, 0:128],
                                                   rows8[0:8, 128:256]), inc="ve")
            tvv = p.tok("ve")
            trow = p.dma("sync", ropeA[0:1, 0:1024], rows8[0:8, 0:128], waits=[tvv],
                         sem="rowd")
            for s_ in range(2):
                p.rec("tensor", lambda e, s_=s_: e.matmul(
                    pY[:, s_ * 512:(s_ + 1) * 512], onesf[0:1, :],
                    ropeA[0:1, s_ * 512:(s_ + 1) * 512],
                    start=True, stop=True), waits=[trow] if s_ == 0 else (), inc="pe")
            tpe = p.tok("pe")
            p.rec("vector", lambda e: e.tensor_copy(gnbg[:], pY[:, 0:1024]), waits=[tpe], inc="ve")
            load_weight_scaled(WREG[4], dram[f"wo{li}"], lambda j: gncol[:, j:j + 1], 8)
            p.rec("vector", lambda e: e.memset(Sst[:], 0.0), inc="ve")
            p.rec("vector", lambda e: e.memset(S16[:], 0.0), inc="ve")
            p.barrier()

            tok_S16 = {h: p.tok("ve") for h in range(H)}
            ln_stats(xs[:, 0:1024], mv[:, 4:5], mv[:, 5:6])
            p.rec("vector", lambda e: e.tensor_scalar(
                xht[:], xs[:, 0:1024], mv[:, 4:5], mv[:, 5:6], ALU.mult, ALU.add),
                waits=[p.tok("pe")], inc="ve")
            tapply = p.tok("ve")
            tnorm_prev = p.tok("ve")
            tktab_prev = p.tok("ve")
            tSupd7_prev = p.tok("ve")
            t_ycopy_prev = p.tok("ac")
            pend = None
            for c in range(NCH):
                xc = xs[:, c * 1024:(c + 1) * 1024]
                tcs = p.dma("sync", coslot, dram["cos_tab"][:, c * 64:(c + 1) * 64],
                            waits=[tktab_prev], sem="cs")
                tcs = p.dma("sync", sinlot, dram["sin_tab"][:, c * 64:(c + 1) * 64],
                            sem="cs")
                transpose8(xht, xhf, extra_waits=[tapply])
                txhf = p.tok("ac")
                if c + 1 < NCH:
                    xn_ = xs[:, (c + 1) * 1024:(c + 2) * 1024]
                    ln_stats(xn_, mv[:, 4:5], mv[:, 5:6])
                    p.rec("vector", lambda e, xn_=xn_: e.tensor_scalar(
                        xht[:], xn_, mv[:, 4:5], mv[:, 5:6], ALU.mult, ALU.add),
                        waits=[txhf], inc="ve")
                    tapply = p.tok("ve")
                xf_tiles = lambda j: xhf[:, j * 128:(j + 1) * 128]
                cosc = coslot[:, 0:64].unsqueeze(1).broadcast_to((128, 8, 64))
                sinc = sinlot[:, 0:64].unsqueeze(1).broadcast_to((128, 8, 64))
                gq3 = gamqS[:, 0:8].unsqueeze(2).broadcast_to((128, 8, 64))
                gk3 = gamk2S[:, 0:8].unsqueeze(2).broadcast_to((128, 8, 64))
                tq3 = tabq[:, 0:512].rearrange("p (h f) -> p h f", h=8)
                tq3b = tabq[:, 512:1024].rearrange("p (h f) -> p h f", h=8)
                tk3 = tabk[:, 0:512].rearrange("p (h f) -> p h f", h=8)
                tk3b = tabk[:, 512:1024].rearrange("p (h f) -> p h f", h=8)
                p.rec("vector", lambda e, a=tq3, g=gq3, cc2=cosc: e.tensor_tensor(a, g, cc2, ALU.mult),
                      waits=[tcs], inc="ve")
                p.rec("vector", lambda e, a=tq3b, g=gq3, ss2=sinc: e.tensor_tensor(a, g, ss2, ALU.mult), inc="ve")
                ttab = p.tok("ve")

                tm_proj(pY[:, 0:512], 0, WREG[0], xf_tiles, 0,
                        extra_waits=[txhf, t_ycopy_prev], auto=False)
                tq = tm_proj(pY[:, 512:1024], 1, WREG[0], xf_tiles, 0, auto=False)
                pv = pY[:, 0:1024].rearrange("p (h s f) -> p h s f", h=8, s=2)
                x1, x2 = pv[:, :, 0, :], pv[:, :, 1, :]
                qv = qtm[:, 0:1024].rearrange("p (h s f) -> p h s f", h=8, s=2)
                q1, q2 = qv[:, :, 0, :], qv[:, :, 1, :]
                rA = ropeA[:, 0:512].rearrange("p (h f) -> p h f", h=8)
                rB = ropeB[:, 0:512].rearrange("p (h f) -> p h f", h=8)
                p.rec("vector", lambda e, x1=x1, rA=rA, t=tq3: e.tensor_tensor(rA, x1, t, ALU.mult),
                      waits=[tq, ttab], inc="ve")
                p.rec("vector", lambda e, x2=x2, rB=rB, t=tq3b: e.tensor_tensor(rB, x2, t, ALU.mult), inc="ve")
                p.rec("vector", lambda e, q1=q1, rA=rA, rB=rB: e.tensor_tensor(q1, rA, rB, ALU.subtract), inc="ve")
                p.rec("vector", lambda e, x1=x1, rA=rA, t=tq3b: e.tensor_tensor(rA, x1, t, ALU.mult), inc="ve")
                p.rec("vector", lambda e, x2=x2, rB=rB, t=tq3: e.tensor_tensor(rB, x2, t, ALU.mult), inc="ve")
                p.rec("vector", lambda e, q2=q2, rA=rA, rB=rB: e.tensor_tensor(q2, rA, rB, ALU.add), inc="ve")
                tropeq = p.tok("ve")
                if stage.rstrip("0123456789") == "dbgq" and li == n_layers - 1:
                    p.rec("vector", lambda e, xc=xc: e.tensor_copy(xc, qtm[:, 0:1024]), inc="ve")
                    tnorm_prev = tktab_prev = tSupd7_prev = p.tok("ve")
                    t_ycopy_prev = p.tok("ac")
                    continue

                p.rec("vector", lambda e, a=tk3, g=gk3, cc2=cosc: e.tensor_tensor(a, g, cc2, ALU.mult), inc="ve")
                p.rec("vector", lambda e, a=tk3b, g=gk3, ss2=sinc: e.tensor_tensor(a, g, ss2, ALU.mult), inc="ve")
                tktab_prev = p.tok("ve")
                tm_proj(pA[:, 0:512], 0, WREG[1], xf_tiles, 1024,
                        extra_waits=[txhf, tSupd7_prev], auto=False)
                tk = tm_proj(pB[:, 0:512], 1, WREG[1], xf_tiles, 1024, auto=False)
                for hf, psrc in ((0, pA), (1, pB)):
                    pvh = psrc[:, 0:512].rearrange("p (h s f) -> p h s f", h=4, s=2)
                    xh1, xh2 = pvh[:, :, 0, :], pvh[:, :, 1, :]
                    kvh = ktm2[:, hf * 512:hf * 512 + 512].rearrange(
                        "p (h s f) -> p h s f", h=4, s=2)
                    kh1, kh2 = kvh[:, :, 0, :], kvh[:, :, 1, :]
                    rAh = ropeA[:, hf * 256:hf * 256 + 256].rearrange("p (h f) -> p h f", h=4)
                    rBh = ropeB[:, hf * 256:hf * 256 + 256].rearrange("p (h f) -> p h f", h=4)
                    t3 = tabk[:, hf * 256:hf * 256 + 256].rearrange("p (h f) -> p h f", h=4)
                    t3b = tabk[:, 512 + hf * 256:512 + hf * 256 + 256].rearrange(
                        "p (h f) -> p h f", h=4)
                    p.rec("vector", lambda e, x1=xh1, rA=rAh, t=t3: e.tensor_tensor(rA, x1, t, ALU.mult),
                          waits=[tk] if hf == 0 else (), inc="ve")
                    p.rec("vector", lambda e, x2=xh2, rB=rBh, t=t3b: e.tensor_tensor(rB, x2, t, ALU.mult), inc="ve")
                    p.rec("vector", lambda e, k1=kh1, rA=rAh, rB=rBh: e.tensor_tensor(k1, rA, rB, ALU.add), inc="ve")
                    p.rec("vector", lambda e, x2=xh2, rA=rAh, t=t3: e.tensor_tensor(rA, x2, t, ALU.mult), inc="ve")
                    p.rec("vector", lambda e, x1=xh1, rB=rBh, t=t3b: e.tensor_tensor(rB, x1, t, ALU.mult), inc="ve")
                    p.rec("vector", lambda e, k2=kh2, rA=rAh, rB=rBh: e.tensor_tensor(k2, rA, rB, ALU.subtract), inc="ve")
                tropek = p.tok("ve")
                g128v = gk128S[:, 0:8].unsqueeze(2).broadcast_to((128, 8, 128))
                p.rec("vector", lambda e, g=g128v: e.tensor_tensor(
                    ktms[:, 0:1024].rearrange("p (h f) -> p h f", h=8),
                    ktm2[:, 0:1024].rearrange("p (h f) -> p h f", h=8), g, ALU.mult), inc="ve")
                tktms = p.tok("ve")
                if stage.rstrip("0123456789") == "dbgk" and li == n_layers - 1:
                    p.rec("vector", lambda e, xc=xc: e.tensor_copy(xc, ktm2[:, 0:1024]), inc="ve")
                    tnorm_prev = tSupd7_prev = p.tok("ve")
                    t_ycopy_prev = p.tok("ac")
                    continue

                tm_proj(pY[:, 0:512], 0, WREG[2], xf_tiles, 2048,
                        extra_waits=[tropeq], auto=False)
                tv = tm_proj(pY[:, 512:1024], 1, WREG[2], xf_tiles, 2048, auto=False)
                p.rec("scalar", lambda e: e.copy(vtm[:, 0:1024], pY[:, 0:1024]),
                      waits=[tv], inc="ac")
                tvtm = p.tok("ac")
                if stage.rstrip("0123456789") == "dbgv" and li == n_layers - 1:
                    p.rec("vector", lambda e, xc=xc: e.tensor_copy(xc, vtm[:, 0:1024]),
                          waits=[tvtm], inc="ve")
                    tnorm_prev = tSupd7_prev = p.tok("ve")
                    t_ycopy_prev = p.tok("ac")
                    continue
                tm_proj(pA[:, 0:512], 0, WREG[3], xf_tiles, 3072,
                        extra_waits=[tropek], auto=False)
                tg = tm_proj(pB[:, 0:512], 1, WREG[3], xf_tiles, 3072, auto=False)
                p.rec("scalar", lambda e: e.activation(gtm[:, 0:512], pA[:, 0:512], AF.Silu),
                      waits=[tg], inc="ac")
                p.rec("scalar", lambda e: e.activation(gtm[:, 512:1024], pB[:, 0:512], AF.Silu),
                      inc="ac")
                tgtm = p.tok("ac")
                if stage.rstrip("0123456789") == "dbgg" and li == n_layers - 1:
                    p.rec("vector", lambda e, xc=xc: e.tensor_copy(xc, gtm[:, 0:1024]),
                          waits=[tgtm], inc="ve")
                    tnorm_prev = tSupd7_prev = p.tok("ve")
                    t_ycopy_prev = p.tok("ac")
                    continue

                if pend is not None:
                    pend()
                    pend = None
                transpose8(qtm, qf, extra_waits=[tropeq])
                tqf = p.tok("ac")
                transpose8(ktm2, k2f, extra_waits=[tropek])
                tkf = p.tok("ac")

                tok_mask = {}
                tok_Y = {}
                tok_Supd = {}
                for h in range(H):
                    patt = pAt1 if h % 2 == 0 else pAt2
                    aslot = attm2[:, (h % 2) * 128:(h % 2) * 128 + 128]
                    w_att = [tqf, tkf]
                    if h >= 2:
                        w_att.append(tok_mask[h - 2])
                    else:
                        w_att.append(p.tok("ve"))
                    p.rec("tensor", lambda e, h=h, patt=patt: e.matmul(
                        patt[:, 0:128], k2f[:, h * 128:(h + 1) * 128],
                        qf[:, h * 128:(h + 1) * 128], start=True, stop=True),
                        waits=w_att, inc="pe")
                    tattT = p.tok("pe")
                    p.rec("vector", lambda e, patt=patt, a=aslot: e.tensor_tensor(
                        a, patt[:, 0:128], maskS[:], ALU.mult),
                        waits=[tattT], inc="ve")
                    tok_mask[h] = p.tok("ve")
                    wy = [tok_mask[h], tvtm]
                    p.rec("tensor", lambda e, h=h, a=aslot, c=c: e.matmul(
                        pY[:, h * 128:(h + 1) * 128], a,
                        vtm[:, h * 128:(h + 1) * 128], start=True, stop=(c == 0)),
                        waits=wy, inc="pe")
                    if c > 0:
                        p.rec("tensor", lambda e, h=h: e.matmul(
                            pY[:, h * 128:(h + 1) * 128], qf[:, h * 128:(h + 1) * 128],
                            S16[:, h * 128:(h + 1) * 128], start=False, stop=True),
                            waits=[tok_S16[h]], inc="pe")
                    tok_Y[h] = p.tok("pe")
                    pst = pA if h % 2 == 0 else pB
                    ws = [tktms, tvtm, tgtm]
                    if h >= 2:
                        ws.append(tok_Supd[h - 2])
                    else:
                        ws.append(p.tok("ve"))
                    p.rec("tensor", lambda e, h=h, pst=pst: e.matmul(
                        pst[:, 0:128], ktms[:, h * 128:(h + 1) * 128],
                        vtm[:, h * 128:(h + 1) * 128], start=True, stop=True),
                        waits=ws, inc="pe")
                    tT = p.tok("pe")
                    gch = tabs["gc128"][h]
                    p.rec("vector", lambda e, h=h, pst=pst, gch=gch: e.scalar_tensor_tensor(
                        Sst[:, h * 128:(h + 1) * 128], Sst[:, h * 128:(h + 1) * 128],
                        gch, pst[:, 0:128], ALU.mult, ALU.add), waits=[tT], inc="ve")
                    tok_Supd[h] = p.tok("ve")
                    p.rec("vector", lambda e, h=h: e.tensor_copy(
                        S16[:, h * 128:(h + 1) * 128], Sst[:, h * 128:(h + 1) * 128]),
                        waits=[tok_Y[h]], inc="ve")
                    tok_S16[h] = p.tok("ve")

                tSupd7_prev = tok_Supd[H - 1]
                wgn = [tok_Y[h] for h in range(H)]
                if stage.rstrip("0123456789") == "dbgy" and li == n_layers - 1:
                    p.rec("vector", lambda e, xc=xc: e.tensor_copy(xc, pY[:, 0:1024]), waits=wgn, inc="ve")
                    tnorm_prev = p.tok("ve")
                    t_ycopy_prev = p.tok("ac")
                    continue
                p.rec("scalar", lambda e: e.copy(xnf[:, 0:1024], pY[:, 0:1024]),
                      waits=wgn + [tnorm_prev], inc="ac")
                t_ycopy_prev = p.tok("ac")
                p.rec("scalar", lambda e: e.activation(sqscr[:, 0:1024], xnf[:, 0:1024],
                                                       AF.Square), inc="ac")
                tsq = p.tok("ac")
                pv3 = xnf[:, 0:1024].rearrange("p (h f) -> p h f", h=8)
                sq3 = sqscr[:, 0:1024].rearrange("p (h f) -> p h f", h=8)
                p.rec("vector", lambda e, a=pv3: e.tensor_reduce(gnm[:, 0:8], a, AX.X, ALU.add),
                      waits=[t_ycopy_prev], inc="ve")
                p.rec("vector", lambda e, a=sq3: e.tensor_reduce(gnm[:, 8:16], a, AX.X, ALU.add),
                      waits=[tsq], inc="ve")
                p.rec("vector", lambda e: e.tensor_single_scalar(gnm[:, 0:8], gnm[:, 0:8],
                                                                 1.0 / 128, ALU.mult), inc="ve")
                p.rec("vector", lambda e: e.tensor_mul(statsS[:, 0:8], gnm[:, 0:8], gnm[:, 0:8]), inc="ve")
                p.rec("vector", lambda e: e.scalar_tensor_tensor(
                    gnm[:, 8:16], gnm[:, 8:16], 1.0 / 128, statsS[:, 0:8],
                    ALU.mult, ALU.subtract), inc="ve")
                p.rec("vector", lambda e: e.tensor_single_scalar(gnm[:, 8:16], gnm[:, 8:16],
                                                                 EPS, ALU.add), inc="ve")
                tve = p.tok("ve")
                p.rec("scalar", lambda e: e.activation(gnm[:, 16:24], gnm[:, 8:16], AF.Sqrt),
                      waits=[tve], inc="ac")
                tac = p.tok("ac")
                p.rec("vector", lambda e: e.reciprocal(gnm[:, 16:24], gnm[:, 16:24]),
                      waits=[tac], inc="ve")
                p.rec("vector", lambda e: e.tensor_mul(gnm[:, 24:32], gnm[:, 0:8], gnm[:, 16:24]), inc="ve")
                p.rec("vector", lambda e: e.tensor_single_scalar(gnm[:, 24:32], gnm[:, 24:32],
                                                                 -1.0, ALU.mult), inc="ve")
                for h in range(H):
                    p.rec("vector", lambda e, h=h: e.tensor_scalar(
                        ropeB[:, h * 128:(h + 1) * 128], xnf[:, h * 128:(h + 1) * 128],
                        gnm[:, 16 + h:17 + h], gnm[:, 24 + h:25 + h], ALU.mult, ALU.add), inc="ve")
                tnorm_prev = p.tok("ve")
                p.rec("vector", lambda e: e.tensor_add(ropeA[:, 0:1024], ropeB[:, 0:1024],
                                                       gnbg[:, 0:1024]), inc="ve")
                p.rec("vector", lambda e: e.tensor_mul(gybuf[:, 0:1024], ropeA[:, 0:1024],
                                                       gtm[:, 0:1024]), waits=[tgtm], inc="ve")
                tgy = p.tok("ve")
                DBGWO = (stage.rstrip("0123456789") == "dbgwo" and li == n_layers - 1)
                if stage.rstrip("0123456789") == "dbggy" and li == n_layers - 1:
                    p.rec("vector", lambda e, xc=xc: e.tensor_copy(xc, gybuf[:, 0:1024]), inc="ve")
                    continue

                def mk_epilogue(xc=xc, tgy=tgy, DBGWO=DBGWO, tY7=tok_Y[H - 1]):
                    def epi():
                        for j in range(8):
                            w = [tgy, tY7] if j == 0 else ()
                            p.dma("sync", gyfm[:, j * 128:(j + 1) * 128],
                                  gybuf[:, j * 128:(j + 1) * 128],
                                  waits=w, sem="dgy", transpose=True)
                        tgyf = p.tok("dgy")
                        for half, pp2 in ((0, pA), (1, pB)):
                            for j in range(8):
                                p.rec("tensor", lambda e, j=j, half=half, pp2=pp2: e.matmul(
                                    pp2[:, 0:512], gyfm[:, j * 128:(j + 1) * 128],
                                    WREG[4][:, j * 1024 + half * 512:j * 1024 + half * 512 + 512],
                                    start=(j == 0), stop=(j == 7)),
                                    waits=[tgyf, p.tok("ve"), p.tok("ac")] if j == 0 else (),
                                    inc="pe")
                            tpe = p.tok("pe")
                            if DBGWO:
                                p.rec("vector", lambda e, xc=xc, half=half, pp2=pp2: e.tensor_copy(
                                    xc[:, half * 512:half * 512 + 512], pp2[:, 0:512]),
                                    waits=[tpe], inc="ve")
                            else:
                                p.rec("vector", lambda e, xc=xc, half=half, pp2=pp2: e.scalar_tensor_tensor(
                                    xc[:, half * 512:half * 512 + 512], pp2[:, 0:512], 1.0,
                                    xc[:, half * 512:half * 512 + 512], ALU.mult, ALU.add),
                                    waits=[tpe], inc="ve")
                    return epi

                pend = mk_epilogue()
            if pend is not None:
                pend()
                pend = None
            p.barrier()
            if stage.startswith("dbg") or lupto == "yr":
                return emit_out()

            load_weight_scaled(wpool[:, 0:16384], dram[f"w1{li}"],
                               lambda j: lncol[:, j:j + 1], 8, slab_w=2048)
            load_weight_cast(wpool[:, 16384:32768], dram[f"w2{li}"], 16)
            p.barrier()
            beta_row(wpool[:, 0:16384], 0, width=2048, slab_w=2048)
            p.barrier()
            ln_stats(xs[:, 0:1024], mv[:, 4:5], mv[:, 5:6])
            p.rec("vector", lambda e: e.tensor_scalar(
                xht[:], xs[:, 0:1024], mv[:, 4:5], mv[:, 5:6], ALU.mult, ALU.add),
                waits=[p.tok("pe")], inc="ve")
            tapply = p.tok("ve")
            tres_prev = p.tok("ve")
            for c in range(NCH):
                xc = xs[:, c * 1024:(c + 1) * 1024]
                transpose8(xht, xhf, extra_waits=[tapply])
                txhf = p.tok("ac")
                if c + 1 < NCH:
                    xn_ = xs[:, (c + 1) * 1024:(c + 2) * 1024]
                    ln_stats(xn_, mv[:, 4:5], mv[:, 5:6])
                    p.rec("vector", lambda e, xn_=xn_: e.tensor_scalar(
                        xht[:], xn_, mv[:, 4:5], mv[:, 5:6], ALU.mult, ALU.add),
                        waits=[txhf], inc="ve")
                    tapply = p.tok("ve")
                tok_gelu = {}
                for ft in range(16):
                    pp2 = pA if ft % 2 == 0 else pB
                    w = [txhf]
                    if ft >= 2:
                        w.append(tok_gelu[ft - 2])
                    b0 = 0 + ft * 128
                    p.rec("tensor", lambda e, b0=b0, pp2=pp2: e.matmul(
                        pp2[:, 0:128], rowbuf[0:1, b0:b0 + 128], onesb[0:1, 0:128],
                        start=True, stop=False), waits=w, inc="pe")
                    for j in range(8):
                        p.rec("tensor", lambda e, j=j, ft=ft, pp2=pp2: e.matmul(
                            pp2[:, 0:128], wpool[:, j * 2048 + ft * 128:j * 2048 + ft * 128 + 128],
                            xhf[:, j * 128:(j + 1) * 128], start=False, stop=(j == 7)), inc="pe")
                    tpe = p.tok("pe")
                    p.rec("scalar", lambda e, ft=ft, pp2=pp2: e.activation(
                        h1f[:, ft * 128:(ft + 1) * 128], pp2[:, 0:128], AF.Gelu_apprx_tanh),
                        waits=[tpe], inc="ac")
                    tok_gelu[ft] = p.tok("ac")
                for half in (0, 1):
                    for ft in range(16):
                        if ft == 0:
                            w = [tok_gelu[0 if half == 0 else 15], tres_prev]
                        else:
                            w = [tok_gelu[ft]] if half == 0 else ()
                        p.rec("tensor", lambda e, ft=ft, half=half: e.matmul(
                            pY[:, half * 512:half * 512 + 512],
                            h1f[:, ft * 128:(ft + 1) * 128],
                            wpool[:, 16384 + ft * 1024 + half * 512:16384 + ft * 1024 + half * 512 + 512],
                            start=(ft == 0), stop=(ft == 15)), waits=w, inc="pe")
                    tpe = p.tok("pe")
                    p.rec("vector", lambda e, xc=xc, half=half: e.scalar_tensor_tensor(
                        xc[:, half * 512:half * 512 + 512], pY[:, half * 512:half * 512 + 512],
                        1.0, xc[:, half * 512:half * 512 + 512], ALU.mult, ALU.add),
                        waits=[tpe], inc="ve")
                tres_prev = p.tok("ve")
            p.barrier()

        if stage.startswith("attnA"):
            return emit_out()

        for c in range(NCH):
            buf = ropeA if c % 2 == 0 else ropeB
            w = [p.tok("ve")]
            td = p.dma("sync", buf[:, 0:1024], x0[c * 128:(c + 1) * 128, :], waits=w,
                       sem=f"wl{c % 2}")
            p.rec("vector", lambda e, c=c, buf=buf: e.tensor_add(
                xs[:, c * 1024:(c + 1) * 1024], xs[:, c * 1024:(c + 1) * 1024], buf[:, 0:1024]),
                waits=[td], inc="ve")
        p.barrier()
        if stage == "res1":
            return emit_out()

        def emit_ffn(first):
            if first:
                p.dma("gpsimd", rowbuf[0:1, 3072:4096], dram["bf"][0:1, 0:1024])
                for j in range(8):
                    p.dma("sync", wf32[:, j * 1024:(j + 1) * 1024],
                          dram["wf"][j * 128:(j + 1) * 128, :])
                tw = p.tok("sd")
                p.rec("vector", lambda e: e.tensor_reduce(mv[:, 6:7], wf32[:, 0:8192],
                                                          AX.X, ALU.max,
                                                          apply_absolute_value=True),
                      waits=[tw], inc="ve")
                partition_amax_to_cell(mv[:, 6:7], sclr[0:1, 0:1])
                p.rec("vector", lambda e: e.tensor_single_scalar(sclr[0:1, 0:1], sclr[0:1, 0:1],
                                                                 1e-12, ALU.max), inc="ve")
                p.rec("vector", lambda e: e.reciprocal(sclr[0:1, 1:2], sclr[0:1, 0:1]), inc="ve")
                p.rec("vector", lambda e: e.tensor_single_scalar(sclr[0:1, 1:2], sclr[0:1, 1:2],
                                                                 448.0, ALU.mult), inc="ve")
                bcast_cell_to_col(sclr[0:1, 1:2], scl[:, 0:1])
                for j in range(8):
                    p.rec("vector", lambda e, j=j: e.tensor_copy(xnf[:, 0:1024],
                                                                 wf32[:, j * 1024:(j + 1) * 1024]), inc="ve")
                    quant(xnf[:, 0:1024], scl[:, 0:1], wq8[:, j * 1024:(j + 1) * 1024])
                p.barrier()
            for c in range(NCH):
                xc = xs[:, c * 1024:(c + 1) * 1024]
                ln_stats(xc, stashS[:, 2 * c:2 * c + 1], stashS[:, 2 * c + 1:2 * c + 2])
                p.rec("vector", lambda e, xc=xc, c=c: e.tensor_scalar(
                    xnf[:], xc, stashS[:, 2 * c:2 * c + 1], stashS[:, 2 * c + 1:2 * c + 2],
                    ALU.mult, ALU.add), inc="ve")
                p.rec("vector", lambda e, c=c: e.tensor_reduce(
                    amaxc[:, c:c + 1], xnf[:], AX.X, ALU.max, apply_absolute_value=True), inc="ve")
            p.rec("vector", lambda e: e.tensor_reduce(mv[:, 6:7], amaxc[:, 0:16], AX.X, ALU.max), inc="ve")
            partition_amax_to_cell(mv[:, 6:7], sclr[0:1, 2:3])
            tloc = p.tok("ve")
            tgd = p.dma("gpsimd", cc_in.ap(), sclr[0:1, 2:3], waits=[tloc], sem="ccd")
            p.rec("gpsimd", lambda e: e.collective_compute(
                "AllReduce", ALU.max, replica_groups=[list(range(8))],
                ins=[cc_in.ap()], outs=[cc_out.ap()]), waits=[tgd], inc="cc")
            tcc = p.tok("cc")
            p.rec("gpsimd", None, waits=[tcc])
            tgd = p.dma("gpsimd", sclr[0:1, 3:4], cc_out.ap(), sem="ccd")
            p.rec("vector", lambda e: e.tensor_single_scalar(sclr[0:1, 3:4], sclr[0:1, 3:4],
                                                             1e-12, ALU.max), waits=[tgd], inc="ve")
            p.rec("vector", lambda e: e.reciprocal(sclr[0:1, 4:5], sclr[0:1, 3:4]), inc="ve")
            p.rec("vector", lambda e: e.tensor_single_scalar(sclr[0:1, 4:5], sclr[0:1, 4:5],
                                                             448.0, ALU.mult), inc="ve")
            p.rec("vector", lambda e: e.tensor_mul(sclr[0:1, 5:6], sclr[0:1, 4:5],
                                                   sclr[0:1, 1:2]), inc="ve")
            p.rec("vector", lambda e: e.reciprocal(sclr[0:1, 6:7], sclr[0:1, 5:6]), inc="ve")
            bcast_cell_to_col(sclr[0:1, 4:5], scl[:, 1:2])
            bcast_cell_to_col(sclr[0:1, 6:7], scl[:, 2:3])
            p.rec("vector", lambda e: e.tensor_scalar_mul(
                rowbuf[0:1, 2048:3072], rowbuf[0:1, 3072:4096], sclr[0:1, 5:6]), inc="ve")
            p.barrier()
            for c in range(NCH):
                xc = xs[:, c * 1024:(c + 1) * 1024]
                p.rec("vector", lambda e, xc=xc, c=c: e.tensor_scalar(
                    xnf[:], xc, stashS[:, 2 * c:2 * c + 1], stashS[:, 2 * c + 1:2 * c + 2],
                    ALU.mult, ALU.add), waits=[p.tok("pe")], inc="ve")
                quant(xnf[:, 0:1024], scl[:, 1:2], xq8[:, 0:1024])
                tq8 = p.tok("ve")
                transpose8(xq8, xq8f, extra_waits=[tq8])
                txf = p.tok("ac")
                for half, pp2 in ((0, pA), (1, pB)):
                    b0 = 2048 + half * 512
                    p.rec("tensor", lambda e, b0=b0, pp2=pp2: e.matmul(
                        pp2[:, 0:512], onesb[0:1, :], rowbuf[0:1, b0:b0 + 512],
                        start=True, stop=False), waits=[txf, p.tok("ve")], inc="pe")
                    for j in range(8):
                        p.rec("tensor", lambda e, j=j, half=half, pp2=pp2: e.matmul(
                            pp2[:, 0:512], xq8f[:, j * 128:(j + 1) * 128],
                            wq8[:, j * 1024 + half * 512:j * 1024 + half * 512 + 512],
                            start=False, stop=(j == 7)), inc="pe")
                    tpe = p.tok("pe")
                    p.rec("vector", lambda e, xc=xc, half=half, pp2=pp2: e.scalar_tensor_tensor(
                        xc[:, half * 512:half * 512 + 512], pp2[:, 0:512], scl[:, 2:3],
                        xc[:, half * 512:half * 512 + 512], ALU.mult, ALU.add),
                        waits=[tpe], inc="ve")
            p.barrier()

        emit_ffn(True)
        if stage == "ffn1":
            return emit_out()

        t_r = p.dma("sync", rows8[0:8, 0:128], dram["bbq8"], waits=[p.tok("ve")], sem="rowd")
        prep_vec_cols(rows8[0:8, 0:128], bcolS[:, 0:8], tok_in=t_r)
        t_r = p.dma("sync", rows8[0:8, 0:128], dram["bbk8"], waits=[p.tok("ve")], sem="rowd")
        prep_vec_cols(rows8[0:8, 0:128], bcolS[:, 8:16], tok_in=t_r)
        p.dma("gpsimd", rowbuf[0:1, 0:1024], dram["bbv"][0:1, 0:1024])
        p.dma("gpsimd", rowbuf[0:1, 1024:2048], dram["bbo"][0:1, 0:1024])
        p.dma("sync", ones2S, dram["ones2"], waits=[p.tok("ve")], sem="rowd")
        p.dma("sync", sel2S, dram["sel2"], sem="rowd")
        t_r = p.dma("sync", sqscr[:, 0:128], dram["r2"], sem="rowd")
        p.rec("vector", lambda e: e.tensor_single_scalar(r2b[:], sqscr[:, 0:128], 0.125,
                                                         ALU.mult), waits=[t_r], inc="ve")
        p.rec("vector", lambda e: e.memset(kvacc[:, 0:512], 0.0), inc="ve")
        p.barrier()

        def fm_proj_featuremap(wreg, bcol_off, out_fm, c, tfm):
            for oc in range(8):
                tw0 = [tfm, p.tok("ve")]
                for j in range(8):
                    p.rec("tensor", lambda e, j=j, oc=oc, wreg=wreg: e.matmul(
                        pY[:, oc * 128:(oc + 1) * 128],
                        wreg[:, j * 1024 + oc * 128:j * 1024 + oc * 128 + 128],
                        xhf[:, j * 128:(j + 1) * 128],
                        start=(j == 0), stop=(j == 7)), waits=tw0 if j == 0 else (), inc="pe")
                tpe = p.tok("pe")
                p.rec("vector", lambda e, oc=oc, bo=bcol_off: e.tensor_scalar_add(
                    ropeA[:, oc * 128:(oc + 1) * 128], pY[:, oc * 128:(oc + 1) * 128],
                    bcolS[:, bo + oc:bo + oc + 1]), waits=[tpe], inc="ve")
            trw = p.tok("ve")
            p.rec("scalar", lambda e: e.copy(xht[:, 0:1024], ropeA[:, 0:1024]),
                  waits=[trw], inc="ac")
            p.rec("scalar", lambda e: e.activation(sqscr[:, 0:1024], ropeA[:, 0:1024],
                                                   AF.Square, scale=0.125), inc="ac")
            tsq = p.tok("ac")
            for oc in range(8):
                patt = pAt1 if oc % 2 == 0 else pAt2
                p.rec("tensor", lambda e, oc=oc, patt=patt: e.matmul(
                    patt[:, 0:128], r2b[:], xht[:, oc * 128:(oc + 1) * 128],
                    start=True, stop=True), waits=[tsq, p.tok("ve")], inc="pe")
                tproj = p.tok("pe")
                p.rec("tensor", lambda e, oc=oc: e.matmul(
                    pR[0:2, 0:128], ones2S[:, 0:2], sqscr[:, oc * 128:(oc + 1) * 128],
                    start=True, stop=True), inc="pe")
                tsn = p.tok("pe")
                p.rec("vector", lambda e: e.tensor_copy(rows8[0:2, 0:128], pR[0:2, 0:128]),
                      waits=[tsn], inc="ve")
                trows = p.tok("ve")
                p.rec("tensor", lambda e: e.matmul(pR[:, 128:256], sel2S[0:2, :],
                                                   rows8[0:2, 0:128], start=True, stop=True),
                      waits=[trows], inc="pe")
                tbc = p.tok("pe")
                p.rec("vector", lambda e, oc=oc: e.tensor_copy(
                    sqscr[:, oc * 128:(oc + 1) * 128], pR[:, 128:256]),
                    waits=[tbc], inc="ve")
                p.rec("vector", lambda e, oc=oc, patt=patt: e.scalar_tensor_tensor(
                    ropeB[:, oc * 128:(oc + 1) * 128], sqscr[:, oc * 128:(oc + 1) * 128],
                    -0.5, patt[:, 0:128], ALU.mult, ALU.add), waits=[tproj], inc="ve")
                tcomb = p.tok("ve")
                p.rec("scalar", lambda e, oc=oc, o=out_fm: e.activation(
                    o[:, oc * 128:(oc + 1) * 128], ropeB[:, oc * 128:(oc + 1) * 128],
                    AF.Exp), waits=[tcomb], inc="ac")
            return p.tok("ac")

        load_weight_cast(WREG[0], dram["wbk"], 8)
        load_weight_cast(WREG[1], dram["wbv"], 8)
        p.barrier()
        for c in range(NCH):
            xc = xs[:, c * 1024:(c + 1) * 1024]
            ln_stats(xc, mv[:, 4:5], mv[:, 5:6])
            p.rec("vector", lambda e, xc=xc: e.tensor_scalar(
                xht[:], xc, mv[:, 4:5], mv[:, 5:6], ALU.mult, ALU.add),
                waits=[p.tok("pe")], inc="ve")
            tapply = p.tok("ve")
            transpose8(xht, xhf, extra_waits=[tapply])
            tfm = p.tok("ac")
            tkp = fm_proj_featuremap(WREG[0], 8, k2f, c, tfm)
            transpose8(k2f, ktm2, extra_waits=[tkp])
            tktm = p.tok("ac")
            xf_tiles = lambda j: xhf[:, j * 128:(j + 1) * 128]
            tm_proj(pY[:, 0:512], 0, WREG[1], xf_tiles, 0, extra_waits=[tktm])
            tvp = tm_proj(pY[:, 512:1024], 1, WREG[1], xf_tiles, 0)
            p.rec("scalar", lambda e: e.copy(vtm[:, 0:1024], pY[:, 0:1024]),
                  waits=[tvp], inc="ac")
            tvc = p.tok("ac")
            for h in range(NHB):
                pc, odd = h // 2, h % 2
                tp = (0, 64) if odd else None
                tw = [tktm, tvc]
                if h < 2:
                    tw.append(p.tok("ve"))
                p.rec("tensor", lambda e, h=h, pc=pc, odd=odd, tp=tp: e.matmul(
                    pA[odd * 64:odd * 64 + 64, pc * 64:pc * 64 + 64],
                    ktm2[:, h * 64:(h + 1) * 64], vtm[:, h * 64:(h + 1) * 64],
                    start=True, stop=True, tile_position=tp),
                    waits=tw if h < 2 else (), inc="pe")
            tkv = p.tok("pe")
            p.rec("vector", lambda e: e.tensor_add(kvacc[:, 0:512], kvacc[:, 0:512],
                                                   pA[:, 0:512]), waits=[tkv], inc="ve")
        p.barrier()
        p.rec("vector", lambda e: e.memset(kvq[:, 0:1056], 0.0), inc="ve")
        kvv = kvq[:, 0:1056].rearrange("p (pc w) -> p pc w", pc=8)
        kacv = kvacc[:, 0:512].rearrange("p (pc w) -> p pc w", pc=8)
        p.rec("vector", lambda e: e.tensor_copy(kvv[0:64, :, 0:64], kacv[0:64, :, :]), inc="ve")
        p.rec("vector", lambda e: e.tensor_copy(kvv[64:128, :, 66:130], kacv[64:128, :, :]), inc="ve")
        for pc in range(8):
            p.rec("vector", lambda e, pc=pc: e.memset(kvq[0:64, pc * 132 + 64:pc * 132 + 65], 1.0), inc="ve")
            p.rec("vector", lambda e, pc=pc: e.memset(kvq[64:128, pc * 132 + 130:pc * 132 + 131], 1.0), inc="ve")
        load_weight_cast(WREG[3], dram["wbq"], 8)
        load_weight_cast(WREG[4], dram["wbo"], 8)
        p.barrier()
        for c in range(NCH):
            xc = xs[:, c * 1024:(c + 1) * 1024]
            ln_stats(xc, mv[:, 4:5], mv[:, 5:6])
            p.rec("vector", lambda e, xc=xc: e.tensor_scalar(
                xht[:], xc, mv[:, 4:5], mv[:, 5:6], ALU.mult, ALU.add),
                waits=[p.tok("pe")], inc="ve")
            tapply = p.tok("ve")
            transpose8(xht, xhf, extra_waits=[tapply])
            tfm = p.tok("ac")
            tqp = fm_proj_featuremap(WREG[3], 0, qf, c, tfm)
            for rnd in range(2):
                for s_ in range(4):
                    pc = rnd * 4 + s_
                    w = [tqp, p.tok("ve")] if s_ == 0 else ()
                    p.rec("tensor", lambda e, pc=pc, s_=s_: e.matmul(
                        pY[:, s_ * 256:s_ * 256 + 131], qf[:, pc * 128:(pc + 1) * 128],
                        kvq[:, pc * 132:pc * 132 + 131], start=True, stop=True),
                        waits=w, inc="pe")
                trnd = p.tok("pe")
                pYr = pY[:, 0:1024].rearrange("p (s w) -> p s w", s=4)
                p.rec("vector", lambda e: e.tensor_single_scalar(
                    gnm[:, 0:4].unsqueeze(2), pYr[:, :, 64:65], 1e-8, ALU.add),
                    waits=[trnd], inc="ve")
                p.rec("vector", lambda e: e.tensor_single_scalar(
                    gnm[:, 4:8].unsqueeze(2), pYr[:, :, 130:131], 1e-8, ALU.add), inc="ve")
                p.rec("vector", lambda e: e.reciprocal(gnm[:, 0:8], gnm[:, 0:8]), inc="ve")
                att_ev = ktm2[:, rnd * 512:(rnd + 1) * 512].rearrange("p (s w) -> p s w", s=4)
                zE = gnm[:, 0:4].unsqueeze(2).broadcast_to((128, 4, 64))
                zO = gnm[:, 4:8].unsqueeze(2).broadcast_to((128, 4, 64))
                p.rec("vector", lambda e, a=att_ev, z=zE: e.tensor_tensor(
                    a[:, :, 0:64], pYr[:, :, 0:64], z, ALU.mult), inc="ve")
                p.rec("vector", lambda e, a=att_ev, z=zO: e.tensor_tensor(
                    a[:, :, 64:128], pYr[:, :, 66:130], z, ALU.mult), inc="ve")
            tato = p.tok("ve")
            transpose8(ktm2, gyfm, extra_waits=[tato])
            tof = p.tok("ac")
            for half, pp2 in ((0, pA), (1, pB)):
                b0 = 1024 + half * 512
                p.rec("tensor", lambda e, b0=b0, pp2=pp2: e.matmul(
                    pp2[:, 0:512], onesb[0:1, :], rowbuf[0:1, b0:b0 + 512],
                    start=True, stop=False), waits=[tof, p.tok("ve")], inc="pe")
                for j in range(8):
                    p.rec("tensor", lambda e, j=j, half=half, pp2=pp2: e.matmul(
                        pp2[:, 0:512], gyfm[:, j * 128:(j + 1) * 128],
                        WREG[4][:, j * 1024 + half * 512:j * 1024 + half * 512 + 512],
                        start=False, stop=(j == 7)), inc="pe")
                tpe = p.tok("pe")
                p.rec("vector", lambda e, xc=xc, half=half, pp2=pp2: e.scalar_tensor_tensor(
                    xc[:, half * 512:half * 512 + 512], pp2[:, 0:512], 1.0,
                    xc[:, half * 512:half * 512 + 512], ALU.mult, ALU.add),
                    waits=[tpe], inc="ve")
        p.barrier()
        if stage == "attnB":
            return emit_out()

        emit_ffn(False)
        return emit_out()
    return nc


def kernel(**inputs):
    X = np.ascontiguousarray(inputs["L0_attn_a_X"], dtype=np.float32)
    t = _host_tables()
    common = {k: v for k, v in t.items() if k != "gc128"}
    for i in range(L):
        common[f"wq{i}"] = _perm_cols(inputs["L0_attn_a_Wq"][i])
        common[f"wk{i}"] = _perm_cols(inputs["L0_attn_a_Wk"][i])
        for nm, key in [("wv", "L0_attn_a_Wv"), ("wg", "L0_attn_a_Wg"),
                        ("wo", "L0_attn_a_Wo"), ("w1", "L0_attn_a_W1"),
                        ("w2", "L0_attn_a_W2")]:
            common[f"{nm}{i}"] = np.ascontiguousarray(inputs[key][i], dtype=np.float32)
    common["ln_s"] = np.ascontiguousarray(np.asarray(inputs["L0_attn_a_ln_scale"]).reshape(8, 128), dtype=np.float32)
    common["ln_b"] = np.ascontiguousarray(np.asarray(inputs["L0_attn_a_ln_bias"]).reshape(8, 128), dtype=np.float32)
    common["gn_s"] = np.ascontiguousarray(np.asarray(inputs["L0_attn_a_gn_scale"]).reshape(L, 8, 128), dtype=np.float32)
    common["gn_b"] = np.ascontiguousarray(np.asarray(inputs["L0_attn_a_gn_bias"]).reshape(L, 8, 128), dtype=np.float32)
    for nm, key in [("wbq", "L0_attn_b_Wq"), ("wbk", "L0_attn_b_Wk"),
                    ("wbv", "L0_attn_b_Wv"), ("wbo", "L0_attn_b_Wo")]:
        common[nm] = np.ascontiguousarray(np.asarray(inputs[key]).T, dtype=np.float32)
    for nm, key in [("bbv", "L0_attn_b_bv"), ("bbo", "L0_attn_b_bo"), ("bf", "L0_ffn_bias")]:
        common[nm] = np.ascontiguousarray(np.asarray(inputs[key]).reshape(1, D), dtype=np.float32)
    common["bbq8"] = np.ascontiguousarray(np.asarray(inputs["L0_attn_b_bq"]).reshape(8, 128), dtype=np.float32)
    common["bbk8"] = np.ascontiguousarray(np.asarray(inputs["L0_attn_b_bk"]).reshape(8, 128), dtype=np.float32)
    R = np.asarray(inputs["L0_attn_b_R"], dtype=np.float32)
    R2 = np.zeros((128, 128), dtype=np.float32)
    R2[0:64, 0:64] = R
    R2[64:128, 64:128] = R
    common["r2"] = R2
    common["wf"] = np.ascontiguousarray(np.asarray(inputs["L0_ffn_weight"]).T, dtype=np.float32)

    nc = build_program(_STAGE, t)
    in_maps = []
    for b in range(B):
        m = dict(common)
        m["x0"] = np.ascontiguousarray(X[b])
        in_maps.append(m)
    res = run_bass_kernel_spmd(nc, in_maps, core_ids=list(range(B)))
    return np.stack([np.asarray(res.results[b]["out"]) for b in range(B)], axis=0)
